# version 12
# speedup vs baseline: 1.7097x; 1.1443x over previous
import math
from contextlib import ExitStack

import numpy as np
import ml_dtypes

import concourse.bass as bass
import concourse.mybir as mybir
from concourse.bass_utils import run_bass_kernel_spmd

F32 = mybir.dt.float32
BF16 = mybir.dt.bfloat16
AF = mybir.ActivationFunctionType
ALU = mybir.AluOpType
AX = mybir.AxisListType


class CFG:
    TOK = 2560
    WIN = 1024
    D = 1024
    DC = 8
    HEADS = 16
    DQ = 64
    NB = 10
    E = 8
    M = 2048
    EPS = 1e-5
    SPARSE = True
    PHASES = ("ffn1", "attn", "ffn2")

    @property
    def QT(self): return self.TOK // 128
    @property
    def WT(self): return self.WIN // 128
    @property
    def MS(self): return self.M // 128
    @property
    def NQ(self): return self.MS // 4
    @property
    def SROW(self): return self.WIN + 256
    @property
    def PASSES(self):
        out, t = [], self.TOK
        while t > 0:
            p = min(1024, t); out.append(p); t -= p
        return out


def chunks_of(n, c=512):
    out, o = [], 0
    while o < n:
        w = min(c, n - o); out.append((o, w)); o += w
    return out


class Sem:
    def __init__(self, es, nc, name):
        self.h = es.enter_context(nc.semaphore(name))
        self.v = 0
    def inc(self, inst, n=1):
        inst.then_inc(self.h, n); self.v += n; return self.v
    def wait(self, eng, v=None):
        eng.wait_ge(self.h, self.v if v is None else v)


def build(cfg: CFG):
    nc = bass.Bass("TRN2", target_bir_lowering=False)
    TOK, WIN, D, DC, HEADS, DQ, NB, E, M = (cfg.TOK, cfg.WIN, cfg.D, cfg.DC,
                                            cfg.HEADS, cfg.DQ, cfg.NB, cfg.E, cfg.M)
    QT, WT, MS, NQ, SROW = cfg.QT, cfg.WT, cfg.MS, cfg.NQ, cfg.SROW
    PASSES = cfg.PASSES
    NPAIR = HEADS // 2
    PMAX = max(PASSES)
    NTILE_MAX = PMAX // 128

    dt = nc.dram_tensor
    XFM = dt("XFM", [128, DC * TOK], F32, kind="ExternalInput").ap()
    WGQ = dt("WGQ", [E, MS // 2, 128, 8 * 256], F32, kind="ExternalInput").ap()
    WUQ = dt("WUQ", [E, MS // 2, 128, 8 * 256], F32, kind="ExternalInput").ap()
    WOQ = dt("WOQ", [E, NQ, 128, 4 * D], F32, kind="ExternalInput").ap()
    BGT = dt("BGT", [128, E * MS], F32, kind="ExternalInput").ap()
    BUT = dt("BUT", [128, E * MS], F32, kind="ExternalInput").ap()
    BO = dt("BO", [E, D], F32, kind="ExternalInput").ap()
    WRT = dt("WRT", [128, DC * E], F32, kind="ExternalInput").ap()
    WQL = dt("WQL", [NPAIR, 128, DC * 128], F32, kind="ExternalInput").ap()
    WKL = dt("WKL", [NPAIR, 128, DC * 128], F32, kind="ExternalInput").ap()
    WVL = dt("WVL", [NPAIR, 128, DC * 128], F32, kind="ExternalInput").ap()
    WPL = dt("WPL", [NPAIR, 128, DC * 128], F32, kind="ExternalInput").ap()
    WRL = dt("WRL", [2, 128, DC * 128], F32, kind="ExternalInput").ap()
    BQT = dt("BQT", [128, NPAIR], F32, kind="ExternalInput").ap()
    BRT = dt("BRT", [128, 2], F32, kind="ExternalInput").ap()
    BPJ = dt("BPJ", [1, D], F32, kind="ExternalInput").ap()
    BNDR = dt("BNDR", [NB, WIN], F32, kind="ExternalInput").ap()
    IDN = dt("IDN", [128, 128], F32, kind="ExternalInput").ap()
    CIOTA = dt("CIOTA", [128, 512], F32, kind="ExternalInput").ap()
    IDNF = dt("IDNF", [128, 128], F32, kind="ExternalInput").ap()
    CCOL = dt("CCOL", [128, 4], F32, kind="ExternalInput").ap()
    TRIU = dt("TRIU", [128, 128], F32, kind="ExternalInput").ap()
    BOF = dt("BOF", [1, E * D], F32, kind="ExternalInput").ap()
    OUT = dt("OUT", [128, DC * TOK], F32, kind="ExternalOutput").ap()
    RSCR = dt("RSCR", [256, TOK], BF16, kind="Internal").ap()
    ESCR = dt("ESCR", [HEADS, TOK * SROW], BF16, kind="Internal").ap()

    es = ExitStack()
    with es:
        EC = es.enter_context
        x_fm = EC(nc.sbuf_tensor("x_fm", [128, DC * TOK], F32))
        idn_bf = EC(nc.sbuf_tensor("idn_bf", [128, 128], BF16))
        ones1_f = EC(nc.sbuf_tensor("ones1_f", [1, 128], F32))
        onesK_f = EC(nc.sbuf_tensor("onesK_f", [128, 1], F32))
        ones_bf = EC(nc.sbuf_tensor("ones_bf", [128, 128], BF16))
        bgt = EC(nc.sbuf_tensor("bgt", [128, E * MS], F32))
        but = EC(nc.sbuf_tensor("but", [128, E * MS], F32))
        bo_sb = EC(nc.sbuf_tensor("bo_sb", [E, D], BF16))
        ciota = EC(nc.sbuf_tensor("ciota", [128, 512], F32))
        idn_f = EC(nc.sbuf_tensor("idn_f", [128, 128], F32))
        ccol = EC(nc.sbuf_tensor("ccol", [128, 4], F32))
        triu = EC(nc.sbuf_tensor("triu", [128, 128], F32))
        ones1_bf = EC(nc.sbuf_tensor("ones1_bf", [1, 128], BF16))
        wrt = EC(nc.sbuf_tensor("wrt", [128, DC * E], BF16))
        bqt = EC(nc.sbuf_tensor("bqt", [128, NPAIR], F32))
        brt = EC(nc.sbuf_tensor("brt", [128, 2], F32))
        stat = EC(nc.sbuf_tensor("stat", [1, 2 * PMAX], F32))
        SG = PMAX
        tmp32 = [EC(nc.sbuf_tensor(f"tmp32_{i}", [128, 512], F32)) for i in range(2)]
        sqt = tmp32
        wtok = EC(nc.sbuf_tensor("wtok", [128, 8 * NTILE_MAX], F32))
        rt_sb = EC(nc.sbuf_tensor("rt_sb", [128, 2 * E], F32))
        rvec1 = EC(nc.sbuf_tensor("rvec1", [128, 8], F32))
        if cfg.SPARSE:
            whalf = wtok_bf = None
            wtok_bfa = EC(nc.sbuf_tensor("wtok_bfa", [128, 8 * NTILE_MAX], BF16))
        else:
            whalf = EC(nc.sbuf_tensor("whalf", [E, PMAX], BF16))
            wtok_bf = EC(nc.sbuf_tensor("wtok_bf", [128, E], BF16))
            wtok_bfa = None

        SP3 = EC(nc.psum_tensor("SP3", [128, 1536], F32))
        PA = EC(nc.psum_tensor("PA", [128, 512], F32))
        PBC = EC(nc.psum_tensor("PBC", [128, 1024], F32))
        PTQ = [EC(nc.psum_tensor(f"PTQ{i}", [128, 512], F32)) for i in range(2)]

        class PV:
            def __init__(self, t, off):
                self.t, self.off = t, off
            def __getitem__(self, idx):
                sp, sf = idx
                o0 = self.off + (sf.start or 0)
                o1 = self.off + sf.stop
                return self.t[sp, o0:o1]

        P = [PV(SP3, 0), PV(SP3, 512), PV(SP3, 1024), PV(PA, 0), PV(PBC, 0), PV(PBC, 512)]
        SPAN = (WT + 1) * 128

        dX = Sem(es, nc, "dX")
        dXg = Sem(es, nc, "dXg")
        dW2 = [Sem(es, nc, "dW0"), Sem(es, nc, "dW1")]
        dWO = Sem(es, nc, "dWO")
        dWOb = Sem(es, nc, "dWOb")
        dBF = Sem(es, nc, "dBF")
        dL = Sem(es, nc, "dL")
        dR2 = [Sem(es, nc, "dR0"), Sem(es, nc, "dR1")]
        dG2 = [Sem(es, nc, "dG0"), Sem(es, nc, "dG1")]
        dE2 = [Sem(es, nc, "dE0"), Sem(es, nc, "dE1")]
        dS = Sem(es, nc, "dS")
        dO = Sem(es, nc, "dO")
        T = Sem(es, nc, "T")
        V = Sem(es, nc, "V")
        S = Sem(es, nc, "S")
        WGF = Sem(es, nc, "WGF")
        WOF = Sem(es, nc, "WOF")

        block = EC(nc.Block())
        st = {}

        def pguard(eng, ps):
            g = st.get(("pg", id(ps)))
            if g is not None:
                sem, v = g
                sem.wait(eng, v)
        def pset(ps, sem, v=None):
            st[("pg", id(ps))] = (sem, sem.v if v is None else v)

        def sec(name):
            return getattr(block, name)

        def scope_in(name):
            sid, _ = nc.enter_named_scope(name, False)
            return (name, sid)
        def scope_out(tok):
            nc.leave_named_scope(tok[0], tok[1], False)

        def vdep(v, inst):
            V.inc(inst); V.wait(v)
        def sdep(s, inst):
            S.inc(inst); S.wait(s)

        @sec("sync")
        def _(sy):
            sy.dma_start(out=x_fm[:], in_=XFM).then_inc(dX.h, 16)
            sy.dma_start(out=bgt[:], in_=BGT).then_inc(dX.h, 16)
            sy.dma_start(out=but[:], in_=BUT).then_inc(dX.h, 16)
            sy.dma_start(out=bqt[:], in_=BQT).then_inc(dX.h, 16)
            sy.dma_start(out=brt[:], in_=BRT).then_inc(dX.h, 16)
            sy.dma_start(out=ciota[:], in_=CIOTA).then_inc(dX.h, 16)
            sy.dma_start(out=ccol[:], in_=CCOL).then_inc(dX.h, 16)
        dX.v += 112
        @sec("gpsimd")
        def _(g):
            g.dma_start(out=idn_bf[:], in_=IDN).then_inc(dXg.h, 16)
            g.dma_start(out=bo_sb[:], in_=BO).then_inc(dXg.h, 16)
            g.dma_start(out=wrt[:], in_=WRT).then_inc(dXg.h, 16)
            g.dma_start(out=triu[:], in_=TRIU).then_inc(dXg.h, 16)
            g.dma_start(out=idn_f[:], in_=IDNF).then_inc(dXg.h, 16)
        dXg.v += 80
        X_READY = dX.v
        XG_READY = dXg.v
        @sec("vector")
        def _(v):
            v.memset(ones1_f[:], 1.0)
            v.memset(onesK_f[:], 1.0)
            v.memset(ones_bf[:], 1.0)
            inst = v.memset(ones1_bf[:], 1.0)
            V.inc(inst)
        CONST_READY = V.v

        def phase_join_waits():
            return [(T, T.v), (V, V.v), (S, S.v), (dW2[0], dW2[0].v), (dW2[1], dW2[1].v), (dWO, dWO.v), (dWOb, dWOb.v), (dBF, dBF.v), (dL, dL.v), (dR2[0], dR2[0].v), (dR2[1], dR2[1].v), (dG2[0], dG2[0].v), (dG2[1], dG2[1].v), (dE2[0], dE2[0].v), (dE2[1], dE2[1].v), (dS, dS.v)]

        def wait_join(eng, join):
            for sem, v in join:
                if v > 0:
                    sem.wait(eng, v)

        def emit_ffn(tag, join):
            with ExitStack() as fes:
                FE = fes.enter_context
                NH = MS // 2
                wgq_sb = [FE(nc.sbuf_tensor(f"{tag}_wg{i}", [128, 8 * 256], BF16)) for i in range(2)]
                wuq_sb = [FE(nc.sbuf_tensor(f"{tag}_wu{i}", [128, 8 * 256], BF16)) for i in range(2)]
                SP_COLS = max(DC * PMAX, (NTILE_MAX + DC) * 512) if cfg.SPARSE else DC * PMAX
                xt_sb = FE(nc.sbuf_tensor(f"{tag}_xt", [128, SP_COLS], BF16))
                if cfg.SPARSE:
                    wo_sb = FE(nc.sbuf_tensor(f"{tag}_wo", [128, min(NQ, 2) * 4 * D], BF16))
                    HWS = MS
                    hw_sb = FE(nc.sbuf_tensor(f"{tag}_hw", [128, HWS * 512], BF16))
                    eo_sb = FE(nc.sbuf_tensor(f"{tag}_eo", [128, 4 * 1024], BF16))
                    xtk = FE(nc.sbuf_tensor(f"{tag}_xtk", [128, (PMAX // 128) * D], BF16))
                    sg_sb = FE(nc.sbuf_tensor(f"{tag}_sg", [128, 512], BF16))
                    pmt_sb = FE(nc.sbuf_tensor(f"{tag}_pmt", [128, 4 * 512], BF16))
                    wcol = FE(nc.sbuf_tensor(f"{tag}_wcol", [128, 8], F32))
                    s_all = FE(nc.sbuf_tensor(f"{tag}_sall", [128, 8 * (PMAX // 128)], F32))
                    bases = FE(nc.sbuf_tensor(f"{tag}_bases", [1, 8 * (PMAX // 128 + 1)], F32))
                    m_bf = FE(nc.sbuf_tensor(f"{tag}_mbf", [128, 8], F32))
                    bof_tmp = FE(nc.sbuf_tensor(f"{tag}_bof", [1, D], BF16))
                    wb_sb = None
                else:
                    wo_sb = FE(nc.sbuf_tensor(f"{tag}_wo", [128, NQ * 4 * D], BF16))
                    hw_sb = FE(nc.sbuf_tensor(f"{tag}_hw", [128, (MS // 2) * PMAX], BF16))
                    wb_sb = FE(nc.sbuf_tensor(f"{tag}_wb", [128, PMAX], BF16))

                quarters = [(pi, e, h) for pi in range(len(PASSES)) for e in range(E) for h in range(NH)]
                def emit_gu_dma(k):
                    if k >= len(quarters):
                        return
                    pi, e, q = quarters[k]
                    buf = k % 2
                    dwp = dW2[buf]
                    @sec("gpsimd")
                    def _(g, e=e, q=q, buf=buf, k=k, dwp=dwp):
                        if k < 2:
                            wait_join(g, join)
                        wf = st.get(("wgf_at", k - 2))
                        if wf is not None:
                            WGF.wait(g, wf)
                        g.dma_start(out=wgq_sb[buf][:], in_=WGQ[e, q]).then_inc(dwp.h, 16)
                        g.dma_start(out=wuq_sb[buf][:], in_=WUQ[e, q]).then_inc(dwp.h, 16)
                    dwp.v += 32
                    st[("gu_ld", k)] = (dwp, dwp.v)
                emit_gu_dma(0)
                emit_gu_dma(1)

                kq = 0
                pass_off = 0
                for p_idx, ptok in enumerate(PASSES):
                    pchunks = chunks_of(ptok)
                    ntile = ptok // 128

                    _sc = scope_in(f"{tag}.lnrt{p_idx}")
                    for ci, (co, cw) in enumerate(pchunks):
                        xoff = pass_off + co
                        for dc in range(DC):
                            @sec("scalar")
                            def _(s, ci=ci, dc=dc, xoff=xoff, cw=cw):
                                if ci == 0 and dc == 0:
                                    dX.wait(s, X_READY)
                                    dXg.wait(s, XG_READY)
                                    wait_join(s, join)
                                k = ci * DC + dc
                                wk = st.get((tag, p_idx, "sqfree", k - 2))
                                if wk is not None:
                                    T.wait(s, wk)
                                if dc == 0:
                                    V.wait(s)
                                inst = s.activation(out=sqt[dc % 2][:, 0:cw],
                                                    in_=x_fm[:, dc * TOK + xoff: dc * TOK + xoff + cw],
                                                    func=AF.Square)
                                S.inc(inst)
                                st[(tag, p_idx, "sq", ci, dc)] = S.v
                            @sec("tensor")
                            def _(t, ci=ci, dc=dc, xoff=xoff, cw=cw):
                                if ci == 0 and dc == 0:
                                    dX.wait(t, X_READY)
                                    dXg.wait(t, XG_READY)
                                    V.wait(t, CONST_READY)
                                    wait_join(t, join)
                                    pguard(t, P[4]); pguard(t, P[5])
                                t.matmul(P[4][0:1, 0:cw], onesK_f[:, :],
                                         x_fm[:, dc * TOK + xoff: dc * TOK + xoff + cw],
                                         start=(dc == 0), stop=(dc == DC - 1))
                                S.wait(t, st[(tag, p_idx, "sq", ci, dc)])
                                inst = t.matmul(P[5][0:1, 0:cw], onesK_f[:, :], sqt[dc % 2][:, 0:cw],
                                                start=(dc == 0), stop=(dc == DC - 1))
                                T.inc(inst)
                                st[(tag, p_idx, "sqfree", ci * DC + dc)] = T.v
                        st[(tag, p_idx, "stats", ci)] = T.v
                        @sec("vector")
                        def _(v, ci=ci, co=co, cw=cw):
                            T.wait(v, st[(tag, p_idx, "stats", ci)])
                            V.wait(v)
                            vdep(v, v.tensor_scalar(out=stat[0:1, co:co + cw], in0=P[4][0:1, 0:cw],
                                            scalar1=1.0 / D, scalar2=None, op0=ALU.mult))
                            vdep(v, v.tensor_scalar(out=stat[0:1, SG + co:SG + co + cw], in0=P[5][0:1, 0:cw],
                                            scalar1=1.0 / D, scalar2=None, op0=ALU.mult))
                            vdep(v, v.tensor_tensor(out=tmp32[0][0:1, 0:cw], in0=stat[0:1, co:co + cw],
                                            in1=stat[0:1, co:co + cw], op=ALU.mult))
                            vdep(v, v.tensor_tensor(out=stat[0:1, SG + co:SG + co + cw], in0=stat[0:1, SG + co:SG + co + cw],
                                            in1=tmp32[0][0:1, 0:cw], op=ALU.subtract))
                            inst = v.tensor_scalar(out=stat[0:1, SG + co:SG + co + cw], in0=stat[0:1, SG + co:SG + co + cw],
                                                   scalar1=cfg.EPS, scalar2=None, op0=ALU.add)
                            V.inc(inst)
                            st[(tag, p_idx, "var", ci)] = V.v
                            pset(P[4], V); pset(P[5], V)
                        if ci + 1 < len(pchunks):
                            @sec("tensor")
                            def _(t, ci=ci):
                                V.wait(t, st[(tag, p_idx, "var", ci)])
                    @sec("scalar")
                    def _(s):
                        V.wait(s, st[(tag, p_idx, "var", len(pchunks) - 1)])
                        inst = s.activation(out=stat[0:1, SG:SG + ptok], in_=stat[0:1, SG:SG + ptok], func=AF.Sqrt)
                        S.inc(inst)
                    SQ_D = S.v
                    @sec("vector")
                    def _(v):
                        S.wait(v, SQ_D)
                        inst = v.reciprocal(stat[0:1, SG:SG + ptok], stat[0:1, SG:SG + ptok])
                        V.inc(inst)
                    RSTD_D = V.v

                    for ci, (co, cw) in enumerate(pchunks):
                        xoff = pass_off + co
                        @sec("tensor")
                        def _(t, ci=ci, co=co, cw=cw):
                            if ci == 0:
                                V.wait(t, RSTD_D)
                            pguard(t, P[0]); pguard(t, P[1])
                            t.matmul(P[0][:, 0:cw], ones1_f[:, :], stat[0:1, co:co + cw],
                                     start=True, stop=True)
                            inst = t.matmul(P[1][:, 0:cw], ones1_f[:, :], stat[0:1, SG + co:SG + co + cw],
                                            start=True, stop=True)
                            T.inc(inst)
                            st[(tag, p_idx, "bcast", ci)] = T.v
                        @sec("vector")
                        def _(v, ci=ci, co=co, cw=cw, xoff=xoff):
                            T.wait(v, st[(tag, p_idx, "bcast", ci)])
                            V.wait(v)
                            for dc in range(DC):
                                vdep(v, v.tensor_tensor(out=tmp32[dc % 2][:, 0:cw],
                                                in0=x_fm[:, dc * TOK + xoff: dc * TOK + xoff + cw],
                                                in1=P[0][:, 0:cw], op=ALU.subtract))
                                inst = v.tensor_tensor(out=xt_sb[:, dc * PMAX + co: dc * PMAX + co + cw],
                                                       in0=tmp32[dc % 2][:, 0:cw],
                                                       in1=P[1][:, 0:cw], op=ALU.mult)
                                vdep(v, inst)
                            pset(P[0], V); pset(P[1], V)
                    XT_READY = V.v

                    for ti in range(ntile):
                        toff = ti * 128
                        @sec("tensor")
                        def _(t, ti=ti, toff=toff):
                            if ti == 0:
                                V.wait(t, XT_READY)
                            pguard(t, P[2])
                            for dc in range(DC):
                                inst = t.matmul(P[2][0:128, 0:E],
                                                xt_sb[:, dc * PMAX + toff: dc * PMAX + toff + 128],
                                                wrt[:, dc * E:(dc + 1) * E],
                                                start=(dc == 0), stop=(dc == DC - 1))
                            T.inc(inst)
                            st[("rt", ti)] = T.v
                        @sec("vector")
                        def _(v, ti=ti):
                            T.wait(v, st[("rt", ti)])
                            inst = v.reduce_max(rvec1[:, 0:1], P[2][0:128, 0:E], axis=AX.X, negate=True)
                            V.inc(inst)
                            st[("rtmax", ti)] = V.v
                        @sec("scalar")
                        def _(s, ti=ti):
                            V.wait(s, st[("rtmax", ti)])
                            inst = s.activation(out=rt_sb[:, 0:E], in_=P[2][0:128, 0:E], func=AF.Exp,
                                                bias=rvec1[:, 0:1], accum_out=rvec1[:, 1:2])
                            S.inc(inst)
                            st[("rtexp", ti)] = S.v
                        @sec("vector")
                        def _(v, ti=ti):
                            S.wait(v, st[("rtexp", ti)])
                            V.wait(v)
                            pset(P[2], S, st[("rtexp", ti)])
                            vdep(v, v.reciprocal(rvec1[:, 2:3], rvec1[:, 1:2]))
                            vdep(v, v.tensor_scalar(out=rt_sb[:, 0:E], in0=rt_sb[:, 0:E],
                                            scalar1=rvec1[:, 2:3], scalar2=None, op0=ALU.mult))
                            vdep(v, v.reduce_max(rvec1[:, 3:4], rt_sb[:, 0:E], axis=AX.X))
                            vdep(v, v.tensor_scalar(out=rt_sb[:, E:2 * E], in0=rt_sb[:, 0:E],
                                            scalar1=rvec1[:, 3:4], scalar2=None, op0=ALU.is_lt))
                            vdep(v, v.tensor_tensor(out=rt_sb[:, E:2 * E], in0=rt_sb[:, 0:E],
                                            in1=rt_sb[:, E:2 * E], op=ALU.mult))
                            vdep(v, v.reduce_max(rvec1[:, 4:5], rt_sb[:, E:2 * E], axis=AX.X))
                            vdep(v, v.tensor_tensor(out=rvec1[:, 5:6], in0=rvec1[:, 3:4],
                                            in1=rvec1[:, 4:5], op=ALU.add))
                            vdep(v, v.tensor_scalar(out=rvec1[:, 5:6], in0=rvec1[:, 5:6],
                                            scalar1=1e-8, scalar2=None, op0=ALU.add))
                            vdep(v, v.reciprocal(rvec1[:, 6:7], rvec1[:, 5:6]))
                            vdep(v, v.tensor_scalar(out=rt_sb[:, E:2 * E], in0=rt_sb[:, 0:E],
                                            scalar1=rvec1[:, 4:5], scalar2=None, op0=ALU.is_ge))
                            vdep(v, v.tensor_tensor(out=rt_sb[:, E:2 * E], in0=rt_sb[:, 0:E],
                                            in1=rt_sb[:, E:2 * E], op=ALU.mult))
                            inst = v.tensor_scalar(out=wtok[:, ti * 8:ti * 8 + E], in0=rt_sb[:, E:2 * E],
                                                   scalar1=rvec1[:, 6:7], scalar2=0.5,
                                                   op0=ALU.mult, op1=ALU.mult)
                            V.inc(inst)
                            st[("wtok", ti)] = V.v
                        if cfg.SPARSE:
                            @sec("vector")
                            def _(v, ti=ti):
                                V.wait(v)
                                inst = v.tensor_copy(wtok_bfa[:, ti * 8:ti * 8 + E],
                                                     wtok[:, ti * 8:ti * 8 + E])
                                V.inc(inst)
                        else:
                            @sec("vector")
                            def _(v, ti=ti):
                                if ti > 0:
                                    T.wait(v, st[("whalftp", ti - 1)])
                                V.wait(v)
                                inst = v.tensor_copy(wtok_bf[:], wtok[:, ti * 8:ti * 8 + E])
                                V.inc(inst)
                                st[("wtokbf", ti)] = V.v
                            @sec("tensor")
                            def _(t, ti=ti):
                                V.wait(t, st[("wtokbf", ti)])
                                pguard(t, PTQ[0])
                                inst = t.matmul(PTQ[0][0:E, 0:128], wtok_bf[:], idn_bf[:],
                                                start=True, stop=True)
                                T.inc(inst)
                                st[("whalftp", ti)] = T.v
                            @sec("scalar")
                            def _(s, ti=ti, toff=toff):
                                T.wait(s, st[("whalftp", ti)])
                                inst = s.copy(out=whalf[0:E, toff:toff + 128], in_=PTQ[0][0:E, 0:128])
                                S.inc(inst)
                                pset(PTQ[0], S)
                    WTOK_ALL = st[("wtok", ntile - 1)]
                    WHALF_ALL = S.v

                    if cfg.SPARSE:
                        pexp = 2.0 / E
                        capf = int(2 * ptok / E + 8.0 * (ptok * pexp * (1 - pexp)) ** 0.5)
                        CAP = min(512, -(-capf // 128) * 128)
                        if tag != "ffn1":
                            CAP = min(512, max(CAP, -(-(ptok // 2) // 128) * 128))
                        NCC = CAP // 128
                        ngrp = 2 * NCC
                        XG_OFF = ntile * CAP
                        assert XG_OFF + DC * CAP <= SP_COLS and NCC * 1024 <= SP_COLS

                        def GOUT(grp, w=512):
                            return P[grp][0:128, 0:w] if grp < 6 else PTQ[grp - 6][:, 0:w]

                        def GKEY(grp):
                            return P[grp] if grp < 6 else PTQ[grp - 6]

                        @sec("vector")
                        def _(v):
                            V.wait(v)
                            inst = v.memset(bases[0:1, 0:E], 0.0)
                            V.inc(inst)
                        st["bs0"] = V.v
                        for ti in range(ntile):
                            @sec("vector")
                            def _(v, ti=ti):
                                V.wait(v, st[("wtok", ti)])
                                wk = st.get(("sp_cnt", ti - 1))
                                if wk is not None:
                                    T.wait(v, wk)
                                V.wait(v)
                                inst = v.tensor_scalar(out=m_bf[:, 0:E], in0=wtok[:, ti * 8:ti * 8 + E],
                                                       scalar1=0.0, scalar2=None, op0=ALU.is_gt)
                                V.inc(inst)
                                st[("sp_m", ti)] = V.v
                            @sec("tensor")
                            def _(t, ti=ti):
                                V.wait(t, st[("sp_m", ti)])
                                pguard(t, P[4])
                                inst = t.matmul(P[4][0:128, 0:E], triu[:, :], m_bf[:, 0:E],
                                                start=True, stop=True)
                                T.inc(inst)
                                pguard(t, P[5])
                                inst = t.matmul(P[5][0:1, 0:E], onesK_f[:, :], m_bf[:, 0:E],
                                                start=True, stop=True)
                                T.inc(inst)
                                st[("sp_cnt", ti)] = T.v
                                pguard(t, P[3])
                                V.wait(t, st.get(("sp_bok", ti - 1), st["bs0"]))
                                inst = t.matmul(P[3][0:128, 0:E], ones1_f[:, :],
                                                bases[0:1, ti * 8:ti * 8 + E], start=True, stop=True)
                                T.inc(inst)
                                st[("sp_bb", ti)] = T.v
                            @sec("vector")
                            def _(v, ti=ti):
                                T.wait(v, st[("sp_bb", ti)])
                                V.wait(v)
                                vdep(v, v.tensor_copy(s_all[:, ti * 8:ti * 8 + E], P[4][0:128, 0:E]))
                                vdep(v, v.tensor_tensor(out=s_all[:, ti * 8:ti * 8 + E],
                                                        in0=s_all[:, ti * 8:ti * 8 + E],
                                                        in1=P[3][0:128, 0:E], op=ALU.add))
                                vdep(v, v.tensor_scalar(out=s_all[:, ti * 8:ti * 8 + E],
                                                        in0=s_all[:, ti * 8:ti * 8 + E],
                                                        scalar1=1e6, scalar2=None, op0=ALU.add))
                                vdep(v, v.scalar_tensor_tensor(out=s_all[:, ti * 8:ti * 8 + E],
                                                               in0=m_bf[:, 0:E], scalar=-1e6,
                                                               in1=s_all[:, ti * 8:ti * 8 + E],
                                                               op0=ALU.mult, op1=ALU.add))
                                vdep(v, v.tensor_tensor(out=bases[0:1, (ti + 1) * 8:(ti + 1) * 8 + E],
                                                        in0=bases[0:1, ti * 8:ti * 8 + E],
                                                        in1=P[5][0:1, 0:E], op=ALU.add))
                                pset(P[4], V); pset(P[5], V); pset(P[3], V)
                                st[("sp_bok", ti)] = V.v
                        SALL_RDY = st[("sp_bok", ntile - 1)]
                        for ti in range(ntile):
                            for gg in range(2):
                                bank = PTQ[gg]
                                @sec("tensor")
                                def _(t, ti=ti, gg=gg, bank=bank):
                                    if ti == 0 and gg == 0:
                                        V.wait(t, XT_READY)
                                    pguard(t, bank)
                                    for dd in range(4):
                                        dc = gg * 4 + dd
                                        inst = t.matmul(bank[:, dd * 128:(dd + 1) * 128],
                                                        xt_sb[:, dc * PMAX + ti * 128: dc * PMAX + ti * 128 + 128],
                                                        idn_bf[:], start=True, stop=True)
                                    T.inc(inst)
                                    st[("xtk_t", ti, gg)] = T.v
                                @sec("vector")
                                def _(v, ti=ti, gg=gg, bank=bank):
                                    T.wait(v, st[("xtk_t", ti, gg)])
                                    inst = v.tensor_copy(xtk[:, ti * D + gg * 512: ti * D + gg * 512 + 512],
                                                         bank[:, 0:512])
                                    V.inc(inst)
                                    pset(bank, V)
                        XTK_RDY = V.v

                        pm_all = xt_sb
                        EVORD = [g for g in (4, 5, 3) if g < ngrp] + \
                                [g for g in range(ngrp) if g not in (4, 5, 3)]

                        def emit_tail(e, wc_v):
                            wsl = (e % 2) * 4
                            @sec("scalar")
                            def _(s, e=e, wsl=wsl, wc_v=wc_v):
                                for g_i, grp in enumerate(EVORD):
                                    cc, dh = divmod(grp, 2)
                                    osem, oval = st[("sp_op", grp)]
                                    osem.wait(s, oval)
                                    if g_i == 0:
                                        V.wait(s, wc_v)
                                    inst = s.activation(out=eo_sb[:, cc * 1024 + dh * 512: cc * 1024 + dh * 512 + 512],
                                                        in_=GOUT(grp), func=AF.Copy,
                                                        scale=wcol[:, wsl + cc:wsl + cc + 1])
                                    S.inc(inst)
                                    pset(GKEY(grp), S)
                            EO_RDY = S.v
                            st[("eo_rdy_slot", e % 2)] = S.v
                            @sec("tensor")
                            def _(t, e=e):
                                pguard(t, P[4]); pguard(t, P[5])
                                V.wait(t, st[("sp_xg", DC - 1)])
                                for ti in range(ntile):
                                    dst = (P[4][0:1, ti * 128: ti * 128 + 128] if ti < 4 else
                                           P[5][0:1, (ti - 4) * 128: (ti - 4) * 128 + 128])
                                    inst = t.matmul(dst, s_all[:, ti * 8 + e: ti * 8 + e + 1],
                                                    idn_f[:, :], start=True, stop=True)
                                T.inc(inst)
                                st["sp_sr_t"] = T.v
                            @sec("vector")
                            def _(v, e=e, ptok=ptok):
                                T.wait(v, st["sp_sr_t"])
                                V.wait(v)
                                inst = v.tensor_copy(stat[0:1, 0:min(512, ptok)],
                                                     P[4][0:1, 0:min(512, ptok)])
                                V.inc(inst)
                                if ptok > 512:
                                    inst = v.tensor_copy(stat[0:1, 512:ptok], P[5][0:1, 0:ptok - 512])
                                    V.inc(inst)
                                pset(P[4], V); pset(P[5], V)
                                st["sp_sr"] = V.v
                            for td, (co, cw) in enumerate(chunks_of(ptok)):
                                @sec("tensor")
                                def _(t, e=e, td=td, co=co, cw=cw):
                                    V.wait(t, st["sp_sr"])
                                    pguard(t, PTQ[0])
                                    inst = t.matmul(PTQ[0][:, 0:cw], ones1_f[:, :],
                                                    stat[0:1, co:co + cw], start=True, stop=True)
                                    T.inc(inst)
                                    st[("sp_br", td)] = T.v
                                @sec("vector")
                                def _(v, e=e, td=td, cw=cw):
                                    T.wait(v, st[("sp_br", td)])
                                    wk = st.get("sp_pmt_free")
                                    if wk is not None:
                                        T.wait(v, wk)
                                    V.wait(v)
                                    for cc in range(NCC):
                                        inst = v.tensor_scalar(out=pmt_sb[:, cc * 512:cc * 512 + cw],
                                                               in0=PTQ[0][:, 0:cw],
                                                               scalar1=ccol[:, cc:cc + 1],
                                                               scalar2=None, op0=ALU.is_equal)
                                        V.inc(inst)
                                    pset(PTQ[0], V)
                                    st[("sp_pmt", td)] = V.v
                                for dc in range(DC):
                                    bank = P[dc % 4]
                                    @sec("tensor")
                                    def _(t, e=e, td=td, dc=dc, bank=bank, cw=cw, EO_RDY=EO_RDY):
                                        V.wait(t, st[("sp_pmt", td)])
                                        if dc == 0:
                                            S.wait(t, EO_RDY)
                                        wk = st.get(("sp_ev", dc - 4))
                                        if wk is not None:
                                            V.wait(t, wk)
                                        pguard(t, bank)
                                        for cc in range(NCC):
                                            inst = t.matmul(bank[:, 0:cw],
                                                            eo_sb[:, cc * 1024 + dc * 128: cc * 1024 + dc * 128 + 128],
                                                            pmt_sb[:, cc * 512:cc * 512 + cw],
                                                            start=(cc == 0), stop=(cc == NCC - 1))
                                        T.inc(inst)
                                        st[("sp_sc", td, dc)] = T.v
                                        if dc == DC - 1:
                                            st["sp_pmt_free"] = T.v
                                    @sec("vector")
                                    def _(v, e=e, td=td, dc=dc, bank=bank, co=co, cw=cw, pass_off=pass_off):
                                        T.wait(v, st[("sp_sc", td, dc)])
                                        V.wait(v)
                                        xoff = pass_off + co
                                        inst = v.tensor_tensor(out=x_fm[:, dc * TOK + xoff: dc * TOK + xoff + cw],
                                                               in0=x_fm[:, dc * TOK + xoff: dc * TOK + xoff + cw],
                                                               in1=bank[:, 0:cw], op=ALU.add)
                                        V.inc(inst)
                                        pset(bank, V)
                                        st[("sp_ev", dc)] = V.v
                                for dc in range(DC):
                                    st.pop(("sp_ev", dc), None)
                            st[("gcons", id(P[0]))] = (V, V.v)
                            st[("gcons", id(P[1]))] = (V, V.v)

                        pending = None
                        for e in range(E):
                            eg = kq // NH
                            wsl = (e % 2) * 4

                            def emit_wo_dma(q, e=e, eg=eg):
                                buf = q % 2
                                dsem = dWO if buf == 0 else dWOb
                                @sec("gpsimd")
                                def _(g, q=q, buf=buf, e=e, eg=eg):
                                    wk = None
                                    if q >= 2:
                                        wk = st.get((tag, "wo_at", eg, q - 2))
                                    else:
                                        cand = [c for c in range(NQ) if c % 2 == buf]
                                        if cand:
                                            wk = st.get((tag, "wo_at", eg - 1, cand[-1]))
                                    if wk is not None:
                                        wk[0].wait(g, wk[1])
                                    g.dma_start(out=wo_sb[:, buf * 4 * D:(buf + 1) * 4 * D],
                                                in_=WOQ[e, q]).then_inc(dsem.h, 16)
                                dsem.v += 16
                                st[(tag, "wo_ld", eg, q)] = (dsem, dsem.v)
                            emit_wo_dma(0)
                            if NQ > 1:
                                emit_wo_dma(1)

                            @sec("vector")
                            def _(v, e=e):
                                V.wait(v, SALL_RDY)
                                V.wait(v, XTK_RDY)
                                if e == 0:
                                    T.wait(v)
                                else:
                                    T.wait(v, st["sp_wc_t"])
                                V.wait(v)
                                for ti in range(ntile):
                                    inst = v.tensor_scalar(out=pm_all[:, ti * CAP: (ti + 1) * CAP],
                                                           in0=ciota[:, 0:CAP],
                                                           scalar1=s_all[:, ti * 8 + e: ti * 8 + e + 1],
                                                           scalar2=None, op0=ALU.is_equal)
                                    V.inc(inst)
                                st[("sp_pm", e)] = V.v
                            eo_base = S.v if pending is not None else None
                            for dc in range(DC):
                                bank = P[4] if dc % 2 == 0 else P[5]
                                grp_b = 4 if dc % 2 == 0 else 5
                                @sec("tensor")
                                def _(t, e=e, dc=dc, bank=bank, grp_b=grp_b, eo_base=eo_base):
                                    if dc == 0:
                                        V.wait(t, st[("sp_pm", e)])
                                    if dc < 2 and eo_base is not None and grp_b < ngrp:
                                        S.wait(t, eo_base + EVORD.index(grp_b) + 1)
                                    pguard(t, bank)
                                    for ti in range(ntile):
                                        inst = t.matmul(bank[:, 0:CAP],
                                                        xtk[:, ti * D + dc * 128: ti * D + dc * 128 + 128],
                                                        pm_all[:, ti * CAP:(ti + 1) * CAP],
                                                        start=(ti == 0), stop=(ti == ntile - 1))
                                    T.inc(inst)
                                    st[("sp_ga", dc)] = T.v
                                @sec("vector")
                                def _(v, e=e, dc=dc, bank=bank):
                                    T.wait(v, st[("sp_ga", dc)])
                                    inst = v.tensor_copy(xt_sb[:, XG_OFF + dc * CAP: XG_OFF + (dc + 1) * CAP],
                                                         bank[:, 0:CAP])
                                    V.inc(inst)
                                    pset(bank, V)
                                    st[("sp_xg", dc)] = V.v
                            @sec("tensor")
                            def _(t, e=e, eo_base=eo_base):
                                if eo_base is not None:
                                    S.wait(t, eo_base + EVORD.index(3) + 1)
                                pguard(t, P[3])
                                for cc in range(NCC):
                                    for ti in range(ntile):
                                        inst = t.matmul(P[3][0:128, cc:cc + 1],
                                                        pm_all[:, ti * CAP + cc * 128: ti * CAP + cc * 128 + 128],
                                                        wtok_bfa[:, ti * 8 + e: ti * 8 + e + 1],
                                                        start=(ti == 0), stop=(ti == ntile - 1))
                                T.inc(inst)
                                st["sp_wc_t"] = T.v
                            @sec("vector")
                            def _(v, e=e, wsl=wsl):
                                T.wait(v, st["sp_wc_t"])
                                wk = st.get(("eo_rdy_slot", e % 2))
                                if wk is not None:
                                    S.wait(v, wk)
                                V.wait(v)
                                inst = v.tensor_copy(wcol[:, wsl:wsl + NCC], P[3][0:128, 0:NCC])
                                V.inc(inst)
                                pset(P[3], V)
                                st["sp_wc"] = V.v

                            if pending is not None:
                                emit_tail(*pending)
                            pending = (e, st["sp_wc"])

                            @sec("gpsimd")
                            def _(g, e=e):
                                wk = st.get("sp_bof_used")
                                if wk is not None:
                                    T.wait(g, wk)
                                g.dma_start(out=bof_tmp[:], in_=BOF[0:1, e * D:(e + 1) * D]).then_inc(dBF.h, 16)
                            dBF.v += 16
                            BOF_RD = dBF.v

                            s0 = S.v
                            v0 = V.v
                            sg_pred = {m: s0 + m + 1 for m in range(MS)}
                            hw_pred = {m: v0 + m + 1 for m in range(MS)}

                            def emit_oproj_chunk(q, e=e, eg=eg, BOF_RD=BOF_RD):
                                @sec("tensor")
                                def _(t, q=q, e=e, eg=eg):
                                    ldsem, ldval = st[(tag, "wo_ld", eg, q)]
                                    ldsem.wait(t, ldval)
                                    if q == 0:
                                        V.wait(t, hw_pred[MS - 1])
                                        dBF.wait(t, BOF_RD)
                                    buf = q % 2
                                    inst = None
                                    for grp in range(ngrp):
                                        cc, dh = divmod(grp, 2)
                                        if q == 0:
                                            pguard(t, GKEY(grp))
                                            t.matmul(GOUT(grp), ones1_bf[:, :],
                                                     bof_tmp[0:1, dh * 512: dh * 512 + 512],
                                                     start=True, stop=False)
                                        for mt in range(4):
                                            m = q * 4 + mt
                                            inst = t.matmul(GOUT(grp),
                                                            hw_sb[:, (m % HWS) * 512 + cc * 128: (m % HWS) * 512 + cc * 128 + 128],
                                                            wo_sb[:, buf * 4 * D + mt * D + dh * 512: buf * 4 * D + mt * D + dh * 512 + 512],
                                                            start=False, stop=(q == NQ - 1 and mt == 3))
                                        if q == NQ - 1:
                                            if grp == ngrp - 1:
                                                WOF.inc(inst)
                                                st["wof_last"] = WOF.v
                                                st[("sp_op", grp)] = (WOF, WOF.v)
                                            else:
                                                T.inc(inst)
                                                st[("sp_op", grp)] = (T, T.v)
                                    if q < NQ - 1:
                                        T.inc(inst)
                                        st[("sp_ch", q)] = T.v
                                        st[(tag, "wo_at", eg, q)] = (T, T.v)
                                if q + 2 < NQ:
                                    emit_wo_dma(q + 2)

                            for m in range(MS):
                                q2, msq = divmod(m, 2)
                                k = kq + q2
                                buf = k % 2
                                @sec("tensor")
                                def _(t, m=m, msq=msq, k=k, buf=buf, e=e):
                                    ldsem, ldval = st[("gu_ld", k)]
                                    ldsem.wait(t, ldval)
                                    if m == 0:
                                        V.wait(t, st[("sp_xg", DC - 1)])
                                    gw = st.get(("gcons", id(P[0])))
                                    if gw is not None:
                                        gw[0].wait(t, gw[1])
                                    pguard(t, P[0])
                                    for dc in range(DC):
                                        t.matmul(P[0][:, 0:CAP],
                                                 wgq_sb[buf][:, dc * 256 + msq * 128: dc * 256 + msq * 128 + 128],
                                                 xt_sb[:, XG_OFF + dc * CAP: XG_OFF + dc * CAP + CAP],
                                                 start=(dc == 0), stop=(dc == DC - 1))
                                    uw = st.get(("gcons", id(P[1])))
                                    if uw is not None:
                                        uw[0].wait(t, uw[1])
                                    pguard(t, P[1])
                                    for dc in range(DC):
                                        inst = t.matmul(P[1][:, 0:CAP],
                                                        wuq_sb[buf][:, dc * 256 + msq * 128: dc * 256 + msq * 128 + 128],
                                                        xt_sb[:, XG_OFF + dc * CAP: XG_OFF + dc * CAP + CAP],
                                                        start=(dc == 0), stop=(dc == DC - 1))
                                    st[("gcons", id(P[0]))] = (S, sg_pred[m])
                                    st[("gcons", id(P[1]))] = (V, hw_pred[m])
                                    if msq == 1:
                                        WGF.inc(inst)
                                        st[("wgf_at", k)] = WGF.v
                                        st[("gup", m)] = (WGF, WGF.v)
                                    else:
                                        T.inc(inst)
                                        st[("gup", m)] = (T, T.v)
                                if msq == 1:
                                    emit_gu_dma(k + 2)
                            @sec("scalar")
                            def _(s, e=e):
                                for m in range(MS):
                                    gsem, gval = st[("gup", m)]
                                    gsem.wait(s, gval)
                                    V.wait(s, hw_pred[m] - 1)
                                    inst = s.activation(out=sg_sb[:, 0:CAP], in_=P[0][:, 0:CAP], func=AF.Silu,
                                                        bias=bgt[:, e * MS + m: e * MS + m + 1])
                                    S.inc(inst)
                            assert S.v == s0 + MS
                            @sec("vector")
                            def _(v, e=e):
                                V.wait(v)
                                for m in range(MS):
                                    S.wait(v, sg_pred[m])
                                    V.wait(v)
                                    inst = v.scalar_tensor_tensor(out=hw_sb[:, (m % HWS) * 512: (m % HWS) * 512 + CAP],
                                                                  in0=P[1][:, 0:CAP],
                                                                  scalar=but[:, e * MS + m: e * MS + m + 1],
                                                                  in1=sg_sb[:, 0:CAP],
                                                                  op0=ALU.add, op1=ALU.mult)
                                    V.inc(inst)
                            assert V.v == v0 + MS
                            st[("gcons", id(P[0]))] = (S, s0 + MS)
                            st[("gcons", id(P[1]))] = (V, v0 + MS)

                            for q in range(NQ):
                                emit_oproj_chunk(q)
                            st[(tag, "wo_at", eg, NQ - 1)] = (WOF, st["wof_last"])
                            st["sp_bof_used"] = T.v
                            kq += NH
                        emit_tail(*pending)
                        scope_out(_sc)

                    else:
                        scope_out(_sc)
                        _sc = scope_in(f"{tag}.exp{p_idx}")
                        for e in range(E):
                            @sec("gpsimd")
                            def _(g, e=e):
                                if tag == "ffn1" and p_idx == 0 and e == 0:
                                    wait_join(g, join)
                                wf = st.get("wof_last")
                                if wf is not None:
                                    WOF.wait(g, wf)
                                for q in range(NQ):
                                    g.dma_start(out=wo_sb[:, q * 4 * D:(q + 1) * 4 * D],
                                                in_=WOQ[e, q]).then_inc(dWO.h, 16)
                            dWO.v += 16 * NQ
                            WO_LD = dWO.v

                            @sec("vector")
                            def _(v, e=e):
                                wb_tp_prev = st.get("wbtp_last")
                                if wb_tp_prev is not None:
                                    T.wait(v, wb_tp_prev)
                                V.wait(v)
                                for ti in range(ntile):
                                    inst = v.tensor_scalar(out=wb_sb[:, ti * 128:(ti + 1) * 128],
                                                           in0=ones_bf[:],
                                                           scalar1=wtok[:, ti * 8 + e: ti * 8 + e + 1],
                                                           scalar2=None, op0=ALU.mult)
                                V.inc(inst)
                                st[("wrep", e)] = V.v
                            ngrp = (ntile + 3) // 4
                            for grp in range(ngrp):
                                t0g = grp * 4
                                t1g = min(ntile, t0g + 4)
                                bank = PTQ[grp % 2]
                                @sec("tensor")
                                def _(t, e=e, grp=grp, t0g=t0g, t1g=t1g, bank=bank):
                                    if grp == 0:
                                        V.wait(t, st[("wrep", e)])
                                    pguard(t, bank)
                                    for ti in range(t0g, t1g):
                                        inst = t.matmul(bank[:, (ti - t0g) * 128:(ti - t0g + 1) * 128],
                                                        wb_sb[:, ti * 128:(ti + 1) * 128], idn_bf[:],
                                                        start=True, stop=True)
                                    T.inc(inst)
                                    st[("wbtp", grp)] = T.v
                                    st["wbtp_last"] = T.v
                                @sec("scalar")
                                def _(s, e=e, grp=grp, t0g=t0g, t1g=t1g, bank=bank):
                                    T.wait(s, st[("wbtp", grp)])
                                    inst = s.copy(out=wb_sb[:, t0g * 128:t1g * 128],
                                                  in_=bank[:, 0:(t1g - t0g) * 128])
                                    S.inc(inst)
                                    pset(bank, S)
                                st[("wbev", e)] = S.v

                            nch = len(pchunks)
                            def emit_opev(mh_, pass_off=pass_off, pchunks=pchunks):
                                @sec("vector")
                                def _(v, mh_=mh_):
                                  V.wait(v)
                                  OROT = [P[4], P[5], P[2]]
                                  for dc in range(DC):
                                    ops = OROT[dc % 3]
                                    for ci, (co, cw) in enumerate(pchunks):
                                        xoff = pass_off + co
                                        osem, oval = st[("op", mh_, ci, dc)]
                                        osem.wait(v, oval)
                                        inst = v.tensor_tensor(out=x_fm[:, dc * TOK + xoff: dc * TOK + xoff + cw],
                                                               in0=x_fm[:, dc * TOK + xoff: dc * TOK + xoff + cw],
                                                               in1=ops[:, 0:cw], op=ALU.add)
                                        V.inc(inst)
                                pset(P[4], V)
                                pset(P[5], V)
                            for mh in range(2):
                              MH = MS // 2
                              s0 = S.v
                              v0 = V.v
                              sg_pred = {}
                              hw_pred = {}
                              idx = 0
                              for mloc in range(MH):
                                m = mh * MH + mloc
                                for ci in range(nch):
                                    idx += 1
                                    sg_pred[(m, ci)] = s0 + idx
                                    hw_pred[(m, ci)] = v0 + 2 * idx
                              for mloc in range(MH):
                                m = mh * MH + mloc
                                q, msq = divmod(m, 4)
                                k = kq + q
                                buf = k % 2
                                @sec("tensor")
                                def _(t, m=m, msq=msq, k=k, buf=buf, e=e, mh=mh, mloc=mloc):
                                    if m == 0:
                                        S.wait(t, st[("wbev", e)])
                                        V.wait(t, WTOK_ALL)
                                    ldsem, ldval = st[("gu_ld", k)]
                                    ldsem.wait(t, ldval)
                                    for ci, (co, cw) in enumerate(pchunks):
                                        gps = P[0] if ci % 2 == 0 else P[2]
                                        ups = P[1] if ci % 2 == 0 else P[3]
                                        pguard(t, gps)
                                        pguard(t, ups)
                                        gw = st.get(("gcons", id(gps)))
                                        if gw is not None:
                                            gw[0].wait(t, gw[1])
                                        for dc in range(DC):
                                            t.matmul(gps[:, 0:cw],
                                                     wgq_sb[buf][:, dc * 512 + msq * 128: dc * 512 + msq * 128 + 128],
                                                     xt_sb[:, dc * PMAX + co: dc * PMAX + co + cw],
                                                     start=(dc == 0), stop=(dc == DC - 1))
                                        uw = st.get(("gcons", id(ups)))
                                        if uw is not None:
                                            uw[0].wait(t, uw[1])
                                        for dc in range(DC):
                                            inst = t.matmul(ups[:, 0:cw],
                                                            wuq_sb[buf][:, dc * 512 + msq * 128: dc * 512 + msq * 128 + 128],
                                                            xt_sb[:, dc * PMAX + co: dc * PMAX + co + cw],
                                                            start=(dc == 0), stop=(dc == DC - 1))
                                        st[("gcons", id(gps))] = (S, sg_pred[(m, ci)])
                                        st[("gcons", id(ups))] = (V, hw_pred[(m, ci)])
                                        if msq == 3 and ci == nch - 1:
                                            WGF.inc(inst)
                                            st[("wgf_at", k)] = WGF.v
                                            st[("gup", m, ci)] = (WGF, WGF.v)
                                        else:
                                            T.inc(inst)
                                            st[("gup", m, ci)] = (T, T.v)
                                if msq == 3:
                                    emit_gu_dma(k + 2)
                              @sec("scalar")
                              def _(s, mh=mh, e=e, pchunks=pchunks):
                                for mloc in range(MS // 2):
                                  m = mh * (MS // 2) + mloc
                                  for ci, (co, cw) in enumerate(pchunks):
                                    gps = P[0] if ci % 2 == 0 else P[2]
                                    gsem, gval = st[("gup", m, ci)]
                                    gsem.wait(s, gval)
                                    inst = s.activation(out=hw_sb[:, mloc * PMAX + co: mloc * PMAX + co + cw],
                                                        in_=gps[:, 0:cw], func=AF.Silu,
                                                        bias=bgt[:, e * MS + m: e * MS + m + 1])
                                    S.inc(inst)
                                    st[("sg", m, ci)] = S.v
                              assert S.v == s0 + idx, (S.v, s0, idx)
                              @sec("vector")
                              def _(v, mh=mh, e=e, pchunks=pchunks):
                                V.wait(v)
                                for mloc in range(MS // 2):
                                  m = mh * (MS // 2) + mloc
                                  for ci, (co, cw) in enumerate(pchunks):
                                    ups = P[1] if ci % 2 == 0 else P[3]
                                    S.wait(v, st[("sg", m, ci)])
                                    V.wait(v)
                                    vdep(v, v.scalar_tensor_tensor(out=tmp32[1][:, 0:cw], in0=ups[:, 0:cw],
                                                           scalar=but[:, e * MS + m: e * MS + m + 1],
                                                           in1=hw_sb[:, mloc * PMAX + co: mloc * PMAX + co + cw],
                                                           op0=ALU.add, op1=ALU.mult))
                                    inst = v.tensor_tensor(out=hw_sb[:, mloc * PMAX + co: mloc * PMAX + co + cw],
                                                           in0=tmp32[1][:, 0:cw],
                                                           in1=wb_sb[:, co:co + cw], op=ALU.mult)
                                    V.inc(inst)
                                    st[("hw", m, ci)] = V.v
                              assert V.v == v0 + 2 * idx, (V.v, v0, idx)
                              pset(P[0], S, st[("sg", mh * MH + MH - 1, 0)])
                              if nch > 1:
                                  pset(P[2], S, st[("sg", mh * MH + MH - 1, 1)])
                              pset(P[1], V, st[("hw", mh * MH + MH - 1, 0)])
                              if nch > 1:
                                  pset(P[3], V, st[("hw", mh * MH + MH - 1, 1)])

                              opev_base = V.v
                              OROT = [P[4], P[5], P[2]]
                              for dc in range(DC):
                                    ops = OROT[dc % 3]
                                    @sec("tensor")
                                    def _(t, dc=dc, ops=ops, e=e, mh=mh, opev_base=opev_base):
                                        if dc == 0:
                                            if mh == 0:
                                                dWO.wait(t, WO_LD)
                                                S.wait(t, WHALF_ALL)
                                            V.wait(t, st[("hw", mh * (MS // 2) + MS // 2 - 1, nch - 1)])
                                        for ci, (co, cw) in enumerate(pchunks):
                                            pguard(t, ops)
                                            first = True
                                            if e == 0 and mh == 0:
                                                t.matmul(ops[:, 0:cw], bo_sb[:, dc * 128:(dc + 1) * 128],
                                                         whalf[:, co:co + cw], start=True, stop=False)
                                                first = False
                                            for ml in range(MS // 2):
                                                mm_ = mh * (MS // 2) + ml
                                                q2, ms2 = divmod(mm_, 4)
                                                inst = t.matmul(
                                                    ops[:, 0:cw],
                                                    wo_sb[:, q2 * 4 * D + ms2 * D + dc * 128: q2 * 4 * D + ms2 * D + dc * 128 + 128],
                                                    hw_sb[:, ml * PMAX + co: ml * PMAX + co + cw],
                                                    start=(first and ml == 0), stop=(ml == MS // 2 - 1))
                                            if mh == 1 and ci == nch - 1 and dc == DC - 1:
                                                WOF.inc(inst)
                                                st["wof_last"] = WOF.v
                                                st[("op", mh, ci, dc)] = (WOF, WOF.v)
                                            else:
                                                T.inc(inst)
                                                st[("op", mh, ci, dc)] = (T, T.v)
                                            st[("pg", id(ops))] = (V, opev_base + dc * nch + ci + 1)
                              emit_opev(mh)
                            kq += NQ
                        scope_out(_sc)

                    pass_off += ptok
            st[f"{tag}_done_V"] = V.v

        def emit_attn(join):
            with ExitStack() as aes:
                AE = aes.enter_context
                xt2 = AE(nc.sbuf_tensor("xt2", [128, DC * TOK], BF16))
                kp = AE(nc.sbuf_tensor("kp", [128, TOK], BF16))
                vp = AE(nc.sbuf_tensor("vp", [128, TOK], BF16))
                qp = AE(nc.sbuf_tensor("qp", [128, TOK], BF16))
                ap_sb = AE(nc.sbuf_tensor("ap_sb", [128, TOK], BF16))
                wl = [AE(nc.sbuf_tensor(f"wl{i}", [128, DC * 128], BF16)) for i in range(2)]
                rh = [AE(nc.sbuf_tensor(f"rh{i}", [NB, TOK], BF16)) for i in range(2)]
                esb = [AE(nc.sbuf_tensor(f"esb{i}", [128, SROW], BF16)) for i in range(2)]
                diag = [AE(nc.sbuf_tensor(f"diag{i}", [128, 128], BF16)) for i in range(2)]
                gsb = [AE(nc.sbuf_tensor(f"gsb{i}", [128, SPAN], BF16)) for i in range(2)]
                lsb = [AE(nc.sbuf_tensor(f"lsb{i}", [128, SPAN], F32)) for i in range(2)]
                psb = [AE(nc.sbuf_tensor(f"psb{i}", [128, max(SPAN, 512)], BF16)) for i in range(2)]
                ptsb = AE(nc.sbuf_tensor("ptsb", [128, SPAN], BF16))
                smx = AE(nc.sbuf_tensor("smx", [128, 8], F32))
                ones_row = AE(nc.sbuf_tensor("ones_row", [1, 512], BF16))
                bpj = AE(nc.sbuf_tensor("bpj", [1, D], BF16))
                bndr = AE(nc.sbuf_tensor("bndr", [NB, WIN], BF16))

                allchunks = chunks_of(TOK)
                NCH = len(allchunks)
                _asc = scope_in("attn.ln2")
                for ci, (co, cw) in enumerate(allchunks):
                    for dc in range(DC):
                        @sec("scalar")
                        def _(s, ci=ci, dc=dc, co=co, cw=cw):
                            if ci == 0 and dc == 0:
                                dX.wait(s, X_READY)
                                dXg.wait(s, XG_READY)
                                wait_join(s, join)
                            k = ci * DC + dc
                            wk = st.get(("a_sqfree", k - 2))
                            if wk is not None:
                                T.wait(s, wk)
                            if dc == 0:
                                V.wait(s)
                            inst = s.activation(out=sqt[dc % 2][:, 0:cw],
                                                in_=x_fm[:, dc * TOK + co: dc * TOK + co + cw],
                                                func=AF.Square)
                            S.inc(inst)
                            st[("a_sq", ci, dc)] = S.v
                        @sec("tensor")
                        def _(t, ci=ci, dc=dc, co=co, cw=cw):
                            if ci == 0 and dc == 0:
                                dX.wait(t, X_READY)
                                dXg.wait(t, XG_READY)
                                V.wait(t, CONST_READY)
                                wait_join(t, join)
                                pguard(t, P[4]); pguard(t, P[5])
                            t.matmul(P[4][0:1, 0:cw], onesK_f[:, :],
                                     x_fm[:, dc * TOK + co: dc * TOK + co + cw],
                                     start=(dc == 0), stop=(dc == DC - 1))
                            S.wait(t, st[("a_sq", ci, dc)])
                            inst = t.matmul(P[5][0:1, 0:cw], onesK_f[:, :], sqt[dc % 2][:, 0:cw],
                                            start=(dc == 0), stop=(dc == DC - 1))
                            T.inc(inst)
                            st[("a_sqfree", ci * DC + dc)] = T.v
                    st[("a_stats", ci)] = T.v
                    @sec("vector")
                    def _(v, ci=ci, cw=cw):
                        T.wait(v, st[("a_stats", ci)])
                        V.wait(v)
                        vdep(v, v.tensor_scalar(out=stat[0:1, 0:cw], in0=P[4][0:1, 0:cw],
                                        scalar1=1.0 / D, scalar2=None, op0=ALU.mult))
                        vdep(v, v.tensor_scalar(out=stat[0:1, SG:SG + cw], in0=P[5][0:1, 0:cw],
                                        scalar1=1.0 / D, scalar2=None, op0=ALU.mult))
                        vdep(v, v.tensor_tensor(out=tmp32[0][0:1, 0:cw], in0=stat[0:1, 0:cw],
                                        in1=stat[0:1, 0:cw], op=ALU.mult))
                        vdep(v, v.tensor_tensor(out=stat[0:1, SG:SG + cw], in0=stat[0:1, SG:SG + cw],
                                        in1=tmp32[0][0:1, 0:cw], op=ALU.subtract))
                        inst = v.tensor_scalar(out=stat[0:1, SG:SG + cw], in0=stat[0:1, SG:SG + cw],
                                               scalar1=cfg.EPS, scalar2=None, op0=ALU.add)
                        V.inc(inst)
                        st[("a_var", ci)] = V.v
                        pset(P[4], V); pset(P[5], V)
                    @sec("scalar")
                    def _(s, ci=ci, cw=cw):
                        V.wait(s, st[("a_var", ci)])
                        inst = s.activation(out=stat[0:1, SG:SG + cw], in_=stat[0:1, SG:SG + cw], func=AF.Sqrt)
                        S.inc(inst)
                        st[("a_sqrt", ci)] = S.v
                    @sec("vector")
                    def _(v, ci=ci, cw=cw):
                        S.wait(v, st[("a_sqrt", ci)])
                        inst = v.reciprocal(stat[0:1, SG:SG + cw], stat[0:1, SG:SG + cw])
                        V.inc(inst)
                        st[("a_rstd", ci)] = V.v
                    @sec("tensor")
                    def _(t, ci=ci, cw=cw):
                        V.wait(t, st[("a_rstd", ci)])
                        pguard(t, P[0]); pguard(t, P[1])
                        t.matmul(P[0][:, 0:cw], ones1_f[:, :], stat[0:1, 0:cw], start=True, stop=True)
                        inst = t.matmul(P[1][:, 0:cw], ones1_f[:, :], stat[0:1, SG:SG + cw], start=True, stop=True)
                        T.inc(inst)
                        st[("a_bcast", ci)] = T.v
                    @sec("vector")
                    def _(v, ci=ci, co=co, cw=cw):
                        T.wait(v, st[("a_bcast", ci)])
                        V.wait(v)
                        for dc in range(DC):
                            vdep(v, v.tensor_tensor(out=tmp32[dc % 2][:, 0:cw],
                                            in0=x_fm[:, dc * TOK + co: dc * TOK + co + cw],
                                            in1=P[0][:, 0:cw], op=ALU.subtract))
                            inst = v.tensor_tensor(out=xt2[:, dc * TOK + co: dc * TOK + co + cw],
                                                   in0=tmp32[dc % 2][:, 0:cw],
                                                   in1=P[1][:, 0:cw], op=ALU.mult)
                            vdep(v, inst)
                        pset(P[0], V); pset(P[1], V)
                XT2_READY = V.v

                scope_out(_asc)
                _asc = scope_in("attn.r")
                @sec("gpsimd")
                def _(g):
                    wait_join(g, join)
                    g.dma_start(out=bpj[:], in_=BPJ).then_inc(dL.h, 16)
                    g.dma_start(out=bndr[:], in_=BNDR).then_inc(dL.h, 16)
                    g.dma_start(out=wl[0][:], in_=WRL[0]).then_inc(dL.h, 16)
                    g.dma_start(out=wl[1][:], in_=WRL[1]).then_inc(dL.h, 16)
                dL.v += 64
                WRL_RD = dL.v
                for oc in range(2):
                    for ci, (co, cw) in enumerate(allchunks):
                        ops = P[4] if (oc * NCH + ci) % 2 == 0 else P[5]
                        @sec("tensor")
                        def _(t, oc=oc, ci=ci, co=co, cw=cw, ops=ops):
                            if oc == 0 and ci == 0:
                                dL.wait(t, WRL_RD)
                                V.wait(t, XT2_READY)
                            pguard(t, ops)
                            for dc in range(DC):
                                inst = t.matmul(ops[:, 0:cw],
                                                wl[oc][:, dc * 128:(dc + 1) * 128],
                                                xt2[:, dc * TOK + co: dc * TOK + co + cw],
                                                start=(dc == 0), stop=(dc == DC - 1))
                            T.inc(inst)
                            st[("r_mm", oc, ci)] = T.v
                        @sec("scalar")
                        def _(s, oc=oc, ci=ci, cw=cw, ops=ops):
                            T.wait(s, st[("r_mm", oc, ci)])
                            dS.wait(s)
                            inst = s.activation(out=esb[0][:, 0:cw], in_=ops[:, 0:cw],
                                                func=AF.Identity, bias=brt[:, oc:oc + 1])
                            S.inc(inst)
                            pset(ops, S)
                            st[("r_ev", oc, ci)] = S.v
                        @sec("sync")
                        def _(sy, oc=oc, ci=ci, co=co, cw=cw):
                            S.wait(sy, st[("r_ev", oc, ci)])
                            sy.dma_start(out=RSCR[oc * 128:(oc + 1) * 128, co:co + cw],
                                         in_=esb[0][:, 0:cw]).then_inc(dS.h, 16)
                        dS.v += 16
                R_STORED = dS.v

                @sec("vector")
                def _(v):
                    dS.wait(v, R_STORED)
                    v.memset(ones_row[:], 1.0)
                    v.memset(esb[0][:], -1e9)
                    inst = v.memset(esb[1][:], -1e9)
                    V.inc(inst)
                ESB_INIT = V.v

                scope_out(_asc)
                for pr in range(NPAIR):
                    @sec("gpsimd")
                    def _(g, pr=pr):
                        if pr == 0:
                            T.wait(g, st[("r_mm", 1, NCH - 1)])
                        else:
                            T.wait(g, st[("pw_done", pr - 1)])
                        g.dma_start(out=wl[0][:], in_=WKL[pr]).then_inc(dL.h, 16)
                        g.dma_start(out=wl[1][:], in_=WVL[pr]).then_inc(dL.h, 16)
                    dL.v += 32
                    KV_RD = dL.v
                    _asc = scope_in(f"attn.kvq{pr}")
                    for ci, (co, cw) in enumerate(allchunks):
                        @sec("tensor")
                        def _(t, pr=pr, ci=ci, co=co, cw=cw):
                            if ci == 0:
                                dL.wait(t, KV_RD)
                                if pr == 0:
                                    V.wait(t, XT2_READY)
                            pguard(t, P[4])
                            for dc in range(DC):
                                inst = t.matmul(P[4][:, 0:cw], wl[0][:, dc * 128:(dc + 1) * 128],
                                                xt2[:, dc * TOK + co: dc * TOK + co + cw],
                                                start=(dc == 0), stop=(dc == DC - 1))
                            T.inc(inst)
                            st[("k_mm", ci)] = T.v
                        @sec("scalar")
                        def _(s, ci=ci, co=co, cw=cw):
                            T.wait(s, st[("k_mm", ci)])
                            inst = s.activation(out=kp[:, co:co + cw], in_=P[4][:, 0:cw],
                                                func=AF.Identity, scale=0.125)
                            S.inc(inst)
                            pset(P[4], S)
                            st[("k_ev", ci)] = S.v
                    for ci, (co, cw) in enumerate(allchunks):
                        @sec("tensor")
                        def _(t, ci=ci, co=co, cw=cw):
                            pguard(t, P[5])
                            for dc in range(DC):
                                inst = t.matmul(P[5][:, 0:cw], wl[1][:, dc * 128:(dc + 1) * 128],
                                                xt2[:, dc * TOK + co: dc * TOK + co + cw],
                                                start=(dc == 0), stop=(dc == DC - 1))
                            T.inc(inst)
                            st[("v_mm", ci)] = T.v
                        @sec("scalar")
                        def _(s, ci=ci, cw=cw):
                            T.wait(s, st[("v_mm", ci)])
                            wk = st.get(("vtmp_free", ci))
                            if wk is not None:
                                T.wait(s, wk)
                            inst = s.copy(out=psb[0][:, 0:cw], in_=P[5][:, 0:cw])
                            S.inc(inst)
                            pset(P[5], S)
                            st[("vtmp_ev", ci)] = S.v
                        bank = PTQ[ci % 2]
                        @sec("tensor")
                        def _(t, ci=ci, cw=cw, bank=bank):
                            S.wait(t, st[("vtmp_ev", ci)])
                            pguard(t, bank)
                            for j in range(cw // 128):
                                inst = t.matmul(bank[:, j * 128:(j + 1) * 128],
                                                psb[0][:, j * 128:(j + 1) * 128], idn_bf[:],
                                                start=True, stop=True)
                            T.inc(inst)
                            st[("v_tp", ci)] = T.v
                            st[("vtmp_free", ci + 1)] = T.v
                        @sec("vector")
                        def _(v, ci=ci, co=co, cw=cw, bank=bank):
                            T.wait(v, st[("v_tp", ci)])
                            inst = v.tensor_copy(vp[:, co:co + cw], bank[:, 0:cw])
                            V.inc(inst)
                            pset(bank, V)
                            st[("v_ev", ci)] = V.v
                    @sec("gpsimd")
                    def _(g, pr=pr):
                        T.wait(g, st[("k_mm", NCH - 1)])
                        g.dma_start(out=wl[0][:], in_=WQL[pr]).then_inc(dL.h, 16)
                    dL.v += 16
                    Q_RD = dL.v
                    for ci, (co, cw) in enumerate(allchunks):
                        @sec("tensor")
                        def _(t, pr=pr, ci=ci, co=co, cw=cw):
                            if ci == 0:
                                dL.wait(t, Q_RD)
                            pguard(t, P[4])
                            for dc in range(DC):
                                inst = t.matmul(P[4][:, 0:cw], wl[0][:, dc * 128:(dc + 1) * 128],
                                                xt2[:, dc * TOK + co: dc * TOK + co + cw],
                                                start=(dc == 0), stop=(dc == DC - 1))
                            T.inc(inst)
                            st[("q_mm", ci)] = T.v
                        @sec("scalar")
                        def _(s, pr=pr, ci=ci, co=co, cw=cw):
                            T.wait(s, st[("q_mm", ci)])
                            inst = s.activation(out=qp[:, co:co + cw], in_=P[4][:, 0:cw],
                                                func=AF.Identity, bias=bqt[:, pr:pr + 1])
                            S.inc(inst)
                            pset(P[4], S)
                            st[("q_ev", ci)] = S.v
                    st[("pw_done", pr)] = st[("q_mm", NCH - 1)]

                    scope_out(_asc)
                    _asc = scope_in(f"attn.ec{pr}")

                    def emit_e_mm(pr_, hh, qt):
                        h = pr_ * 2 + hh
                        @sec("tensor")
                        def _(t, hh=hh, qt=qt):
                            if qt == 0:
                                rs, rv = st[("rh_rd", hh)]
                                rs.wait(t, rv)
                            pguard(t, P[4])
                            for half, (ho, hwid) in enumerate(chunks_of(WIN)):
                                ops = P[4] if half % 2 == 0 else P[5]
                                inst = t.matmul(ops[:, 0:hwid], rh[hh][:, qt * 128:(qt + 1) * 128],
                                                bndr[:, ho:ho + hwid], start=True, stop=True)
                            T.inc(inst)
                            st[("e_mm", h, qt)] = T.v
                            st[("rh_used", hh)] = T.v

                    def emit_e_evac_ops(s, pr_, hh, qt):
                        h = pr_ * 2 + hh
                        par = qt % 2
                        wk = st.get(("e_stored", par, "thru"))
                        if wk is not None:
                            dE2[par].wait(s, wk)
                        V.wait(s, ESB_INIT)
                        T.wait(s, st[("e_mm", h, qt)])
                        inst = s.copy(out=esb[par][:, 128:128 + WIN], in_=PBC[:, 0:WIN])
                        S.inc(inst)
                        pset(P[4], S)
                        pset(P[5], S)
                        st[("e_ev", h, qt)] = S.v

                    def emit_e_store(pr_, hh, qt):
                        h = pr_ * 2 + hh
                        par = qt % 2
                        @sec("sync")
                        def _(sy, h=h, qt=qt, par=par):
                            S.wait(sy, st[("e_ev", h, qt)])
                            sy.dma_start(
                                out=ESCR[h, qt * 128 * SROW:(qt + 1) * 128 * SROW].rearrange("(a b) -> a b", b=SROW),
                                in_=esb[par][:]).then_inc(dE2[par].h, 16)
                        dE2[par].v += 16
                        st[("e_st", h, qt)] = (dE2[par], dE2[par].v)
                        st[("e_stored", par, "thru")] = dE2[par].v

                    def emit_e_qt(pr_, hh, qt):
                        emit_e_mm(pr_, hh, qt)
                        @sec("scalar")
                        def _(s, pr_=pr_, hh=hh, qt=qt):
                            emit_e_evac_ops(s, pr_, hh, qt)
                        emit_e_store(pr_, hh, qt)

                    def emit_rh(pr_, hh):
                        h = pr_ * 2 + hh
                        drp = dR2[hh]
                        @sec("gpsimd")
                        def _(g, h=h, hh=hh, drp=drp):
                            dS.wait(g, R_STORED)
                            wk = st.get(("rh_free", hh))
                            if wk is not None:
                                T.wait(g, wk)
                            g.dma_start(out=rh[hh][:], in_=RSCR[h * NB:(h + 1) * NB, :]).then_inc(drp.h, 16)
                        drp.v += 16
                        st[("rh_rd", hh)] = (drp, drp.v)

                    def emit_core(pr_, hh, eco=None):
                        h = pr_ * 2 + hh
                        hbase = 64 * hh
                        for i in range(QT + 1):
                            if eco is not None and i < QT:
                                emit_e_mm(eco[0], eco[1], i)
                            qt = i
                            kt0 = max(0, qt - WT)
                            nkt = qt - kt0 + 1
                            span = nkt * 128
                            par = qt % 2
                            if i < QT:
                                gbuf = (h * QT + qt) % 2
                                dgp = dG2[gbuf]
                                @sec("gpsimd")
                                def _(g, h=h, qt=qt, gbuf=gbuf, span=span, kt0=kt0, dgp=dgp):
                                    es_, ev_ = st[("e_st", h, qt)]
                                    es_.wait(g, ev_)
                                    wk = st.get(("g_cons", gbuf))
                                    if wk is not None:
                                        V.wait(g, wk)
                                    base = qt * 128 * (SROW - 1) + (127 + WIN) + kt0 * 128
                                    src = ESCR[h, base: base + 128 * (SROW - 1)].rearrange(
                                        "(i j) -> i j", j=SROW - 1)[:, 0:span]
                                    g.dma_start(out=gsb[gbuf][:, 0:span], in_=src).then_inc(dgp.h, 16)
                                dgp.v += 16
                                st[("g_rd", h, qt)] = (dgp, dgp.v)

                            @sec("tensor")
                            def _(t, i=i, hh=hh, h=h, hbase=hbase):
                                if i < QT:
                                    if i == 0 and hh == 0:
                                        S.wait(t, st[("q_ev", NCH - 1)])
                                        S.wait(t, st[("k_ev", NCH - 1)])
                                        V.wait(t, st[("v_ev", NCH - 1)])
                                    wk = st.get("s_psum_guard")
                                    if wk is not None:
                                        V.wait(t, wk)
                                    kt0_ = max(0, i - WT)
                                    off, rem, c_ = 0, (i - kt0_ + 1) * 128, 0
                                    while rem > 0:
                                        w = min(512, rem)
                                        inst = t.matmul(P[c_][:, 0:w],
                                                        qp[hbase:hbase + 64, i * 128:(i + 1) * 128],
                                                        kp[hbase:hbase + 64, kt0_ * 128 + off: kt0_ * 128 + off + w],
                                                        start=True, stop=True)
                                        off += w; rem -= w; c_ += 1
                                    T.inc(inst)
                                    st[("s_mm", h, i)] = T.v
                                if i > 0:
                                    qp_ = i - 1
                                    nkt_ = qp_ - max(0, qp_ - WT) + 1
                                    S.wait(t, st[("p_exp", h, qp_)])
                                    V.wait(t, st[("diag", h, qp_)])
                                    V_base = V.v
                                    for gg in range((nkt_ + 3) // 4):
                                        bank = PTQ[gg % 2]
                                        if gg >= 2:
                                            V.wait(t, V_base + gg - 1)
                                        else:
                                            pguard(t, bank)
                                        for jj in range(gg * 4, min(nkt_, gg * 4 + 4)):
                                            inst = t.matmul(bank[:, (jj - gg * 4) * 128:(jj - gg * 4 + 1) * 128],
                                                            psb[qp_ % 2][:, jj * 128:(jj + 1) * 128],
                                                            diag[qp_ % 2][:], start=True, stop=True)
                                        T.inc(inst)
                                        st[("p_tp_g", h, qp_, gg)] = T.v
                                    st[("p_tp", h, qp_)] = T.v
                                    V.wait(t, V.v + (nkt_ + 3) // 4)
                                    wk = st.get("a_guard_s")
                                    if wk is not None:
                                        S.wait(t, wk)
                                    kt0_ = max(0, qp_ - WT)
                                    for jj in range(nkt_):
                                        kt = kt0_ + jj
                                        inst = t.matmul(P[3][0:64, 0:128],
                                                        vp[:, kt * 128 + 64 * hh: kt * 128 + 64 * hh + 64],
                                                        ptsb[:, jj * 128:(jj + 1) * 128],
                                                        start=(jj == 0), stop=(jj == nkt_ - 1))
                                    T.inc(inst)
                                    st[("av", h, qp_)] = T.v

                            @sec("vector")
                            def _(v, i=i, h=h, hh=hh):
                                if i > 0:
                                    qp_ = i - 1
                                    nkt_ = qp_ - max(0, qp_ - WT) + 1
                                    for gg in range((nkt_ + 3) // 4):
                                        bank = PTQ[gg % 2]
                                        ww = (min(nkt_, gg * 4 + 4) - gg * 4) * 128
                                        T.wait(v, st[("p_tp_g", h, qp_, gg)])
                                        inst = v.tensor_copy(ptsb[:, gg * 512:gg * 512 + ww], bank[:, 0:ww])
                                        V.inc(inst)
                                        pset(bank, V)
                                if i < QT:
                                    qt = i
                                    par = qt % 2
                                    kt0_ = max(0, qt - WT)
                                    span_ = (qt - kt0_ + 1) * 128
                                    gbuf = (h * QT + qt) % 2
                                    T.wait(v, st[("s_mm", h, qt)])
                                    gs_, gv_ = st[("g_rd", h, qt)]
                                    gs_.wait(v, gv_)
                                    if i > 1:
                                        S.wait(v, st[("p_exp", h, i - 2)])
                                    V.wait(v)
                                    vdep(v, v.tensor_tensor(out=lsb[par][:, 0:span_], in0=SP3[:, 0:span_],
                                                    in1=gsb[gbuf][:, 0:span_], op=ALU.add))
                                    inst = v.reduce_max(smx[:, par * 4:par * 4 + 1], lsb[par][:, 0:span_],
                                                        axis=AX.X, negate=True)
                                    V.inc(inst)
                                    st[("g_cons", gbuf)] = V.v
                                    st[("b_done", h, qt)] = V.v
                                    st["s_psum_guard"] = V.v
                                    S.wait(v, S.v + (2 if eco is not None and i < QT else 1))
                                    if i >= 2:
                                        T.wait(v, st[("p_tp", h, i - 2)])
                                    vdep(v, v.reciprocal(smx[:, par * 4 + 2:par * 4 + 3],
                                                         smx[:, par * 4 + 1:par * 4 + 2]))
                                    inst = v.tensor_scalar(out=diag[par][:], in0=idn_bf[:],
                                                           scalar1=smx[:, par * 4 + 2:par * 4 + 3],
                                                           scalar2=None, op0=ALU.mult)
                                    V.inc(inst)
                                    st[("diag", h, qt)] = V.v

                            @sec("scalar")
                            def _(s, i=i, h=h, hh=hh, eco=eco):
                                if eco is not None and i < QT:
                                    emit_e_evac_ops(s, eco[0], eco[1], i)
                                if i < QT:
                                    qt = i
                                    par = qt % 2
                                    span_ = (qt - max(0, qt - WT) + 1) * 128
                                    V.wait(s, st[("b_done", h, qt)])
                                    if i > 1:
                                        T.wait(s, st[("p_tp", h, i - 2)])
                                    inst = s.activation(out=psb[par][:, 0:span_], in_=lsb[par][:, 0:span_], func=AF.Exp,
                                                        bias=smx[:, par * 4:par * 4 + 1],
                                                        accum_out=smx[:, par * 4 + 1:par * 4 + 2])
                                    S.inc(inst)
                                    st[("p_exp", h, qt)] = S.v
                                if i > 0:
                                    qp_ = i - 1
                                    T.wait(s, st[("av", h, qp_)])
                                    inst = s.copy(out=ap_sb[64 * hh:64 * hh + 64, qp_ * 128:(qp_ + 1) * 128],
                                                  in_=P[3][0:64, 0:128])
                                    S.inc(inst)
                                    st["a_guard_s"] = S.v
                                    st[("a_ev", hh, qp_)] = S.v
                            if eco is not None and i < QT:
                                emit_e_store(eco[0], eco[1], i)
                        st[("rh_free", hh)] = st[("rh_used", hh)]

                    if pr == 0:
                        emit_rh(pr, 0)
                        for qt0 in range(QT):
                            emit_e_qt(pr, 0, qt0)
                    emit_rh(pr, 1)
                    emit_core(pr, 0, eco=(pr, 1))
                    if pr + 1 < NPAIR:
                        emit_rh(pr + 1, 0)
                        emit_core(pr, 1, eco=(pr + 1, 0))
                    else:
                        emit_core(pr, 1)

                    scope_out(_asc)
                    _asc = scope_in(f"attn.proj{pr}")
                    @sec("gpsimd")
                    def _(g, pr=pr):
                        T.wait(g, st[("pw_done", pr)])
                        g.dma_start(out=wl[1][:], in_=WPL[pr]).then_inc(dL.h, 16)
                    dL.v += 16
                    WP_RD = dL.v
                    for ci, (co, cw) in enumerate(allchunks):
                        for dc in range(DC):
                            ops = P[4] if dc % 2 == 0 else P[5]
                            @sec("tensor")
                            def _(t, pr=pr, ci=ci, co=co, cw=cw, dc=dc, ops=ops):
                                if ci == 0 and dc == 0:
                                    dL.wait(t, WP_RD)
                                    S.wait(t, st[("a_ev", 1, QT - 1)])
                                pguard(t, ops)
                                first = True
                                if pr == 0:
                                    t.matmul(ops[:, 0:cw], bpj[0:1, dc * 128:(dc + 1) * 128],
                                             ones_row[0:1, 0:cw], start=True, stop=False)
                                    first = False
                                inst = t.matmul(ops[:, 0:cw], wl[1][:, dc * 128:(dc + 1) * 128],
                                                ap_sb[:, co:co + cw], start=first, stop=True)
                                T.inc(inst)
                                st[("pj", ci, dc)] = T.v
                            @sec("vector")
                            def _(v, ci=ci, co=co, cw=cw, dc=dc, ops=ops):
                                T.wait(v, st[("pj", ci, dc)])
                                V.wait(v)
                                inst = v.tensor_tensor(out=x_fm[:, dc * TOK + co: dc * TOK + co + cw],
                                                       in0=x_fm[:, dc * TOK + co: dc * TOK + co + cw],
                                                       in1=ops[:, 0:cw], op=ALU.add)
                                V.inc(inst)
                                pset(ops, V)
                    st[("pw_done", pr)] = T.v
                    scope_out(_asc)
            st["attn_done_V"] = V.v

        join0 = phase_join_waits()
        if "ffn1" in cfg.PHASES:
            emit_ffn("ffn1", join0)
        if "attn" in cfg.PHASES:
            emit_attn(phase_join_waits())
        if "ffn2" in cfg.PHASES:
            emit_ffn("ffn2", phase_join_waits())

        FINAL_V = V.v
        FINAL_T = T.v
        FINAL_S = S.v
        @sec("sync")
        def _(sy):
            V.wait(sy, FINAL_V)
            T.wait(sy, FINAL_T)
            S.wait(sy, FINAL_S)
            sy.dma_start(out=OUT, in_=x_fm[:]).then_inc(dO.h, 16)
            sy.wait_ge(dO.h, 16)
    return nc


def layout_inputs(cfg: CFG, X, Wq, bq, Wk, Wv, Wproj, bproj, Wr, br, b_nd,
                  W_router, W_gate, b_gate, W_up, b_up, W_out, b_out):
    TOK, D, E, M, NB, WIN = cfg.TOK, cfg.D, cfg.E, cfg.M, cfg.NB, cfg.WIN
    DC, MS, NQ, HEADS = cfg.DC, cfg.MS, cfg.NQ, cfg.HEADS
    NPAIR = HEADS // 2
    f32 = np.float32
    d = {}
    d["XFM"] = X.T.reshape(DC, 128, TOK).transpose(1, 0, 2).reshape(128, DC * TOK)
    NH = MS // 2
    d["WGQ"] = W_gate.reshape(E, DC, 128, NH, 256).transpose(0, 3, 2, 1, 4).reshape(E, NH, 128, 8 * 256)
    d["WUQ"] = W_up.reshape(E, DC, 128, NH, 256).transpose(0, 3, 2, 1, 4).reshape(E, NH, 128, 8 * 256)
    d["WOQ"] = W_out.reshape(E, NQ, 4, 128, D).transpose(0, 1, 3, 2, 4).reshape(E, NQ, 128, 4 * D)
    d["BGT"] = b_gate.reshape(E, MS, 128).transpose(2, 0, 1).reshape(128, E * MS)
    d["BUT"] = b_up.reshape(E, MS, 128).transpose(2, 0, 1).reshape(128, E * MS)
    d["BO"] = b_out
    d["WRT"] = W_router.reshape(DC, 128, E).transpose(1, 0, 2).reshape(128, DC * E)
    def wl_layout(W, n_oc):
        return W.reshape(DC, 128, n_oc, 128).transpose(2, 1, 0, 3).reshape(n_oc, 128, DC * 128)
    d["WQL"] = wl_layout(Wq, NPAIR)
    d["WKL"] = wl_layout(Wk, NPAIR)
    d["WVL"] = wl_layout(Wv, NPAIR)
    d["WPL"] = Wproj.reshape(NPAIR, 128, DC * 128)
    Wr_pad = np.zeros((D, 256), f32); Wr_pad[:, 0:HEADS * NB] = Wr
    d["WRL"] = wl_layout(Wr_pad, 2)
    d["BQT"] = bq.reshape(NPAIR, 128).T
    br_pad = np.zeros((256,), f32); br_pad[0:HEADS * NB] = br
    d["BRT"] = br_pad.reshape(2, 128).T
    d["BPJ"] = bproj.reshape(1, D)
    d["BNDR"] = b_nd[:, ::-1]
    d["IDN"] = np.eye(128, dtype=f32)
    d["IDNF"] = np.eye(128, dtype=f32)
    d["CIOTA"] = np.tile(np.arange(512, dtype=f32), (128, 1))
    d["CCOL"] = np.arange(128, dtype=f32)[:, None] + 128.0 * np.arange(4, dtype=f32)[None, :]
    d["TRIU"] = np.triu(np.ones((128, 128), dtype=f32), 1)
    d["BOF"] = b_out.reshape(1, E * D)
    return {k: np.ascontiguousarray(np.asarray(v, f32)) for k, v in d.items()}


def split_args(inputs):
    return dict(
        Wq=np.asarray(inputs["L0_attn_Wq"], np.float32),
        bq=np.asarray(inputs["L0_attn_bq"], np.float32),
        Wk=np.asarray(inputs["L0_attn_Wk"], np.float32),
        Wv=np.asarray(inputs["L0_attn_Wv"], np.float32),
        Wproj=np.asarray(inputs["L0_attn_Wproj"], np.float32),
        bproj=np.asarray(inputs["L0_attn_bproj"], np.float32),
        Wr=np.asarray(inputs["L0_attn_Wr"], np.float32),
        br=np.asarray(inputs["L0_attn_br"], np.float32),
        b_nd=np.asarray(inputs["L0_attn_b_nd"], np.float32),
        W_router=np.asarray(inputs["L0_ffn_W_router"], np.float32),
        W_gate=np.asarray(inputs["L0_ffn_W_gate"], np.float32),
        b_gate=np.asarray(inputs["L0_ffn_b_gate"], np.float32),
        W_up=np.asarray(inputs["L0_ffn_W_up"], np.float32),
        b_up=np.asarray(inputs["L0_ffn_b_up"], np.float32),
        W_out=np.asarray(inputs["L0_ffn_W_out"], np.float32),
        b_out=np.asarray(inputs["L0_ffn_b_out"], np.float32),
    )


def kernel(**inputs):
    cfg = CFG()
    nc = build(cfg)
    X = np.asarray(inputs["L0_attn_X"], np.float32)
    args = split_args(inputs)
    in_maps = []
    for c in range(8):
        b, h = divmod(c, 2)
        rows = X[b, 0:2560] if h == 0 else X[b, 1536:4096]
        in_maps.append(layout_inputs(cfg, np.ascontiguousarray(rows), **args))
    res = run_bass_kernel_spmd(nc, in_maps, core_ids=list(range(8)))
    out = np.empty((4, 4096, 1024), np.float32)
    for c in range(8):
        b, h = divmod(c, 2)
        r = np.asarray(res.results[c]["OUT"]).reshape(128, 8, 2560).transpose(2, 1, 0).reshape(2560, 1024)
        if h == 0:
            out[b, 0:2560] = r
        else:
            out[b, 2560:4096] = r[1024:2560]
    return out


# version 13
# speedup vs baseline: 1.7639x; 1.0317x over previous
import math
from contextlib import ExitStack

import numpy as np
import ml_dtypes

import concourse.bass as bass
import concourse.mybir as mybir
from concourse.bass_utils import run_bass_kernel_spmd

F32 = mybir.dt.float32
BF16 = mybir.dt.bfloat16
AF = mybir.ActivationFunctionType
ALU = mybir.AluOpType
AX = mybir.AxisListType


class CFG:
    TOK = 2560
    WIN = 1024
    D = 1024
    DC = 8
    HEADS = 16
    DQ = 64
    NB = 10
    E = 8
    M = 2048
    EPS = 1e-5
    SPARSE = True
    PHASES = ("ffn1", "attn", "ffn2")

    @property
    def QT(self): return self.TOK // 128
    @property
    def WT(self): return self.WIN // 128
    @property
    def MS(self): return self.M // 128
    @property
    def NQ(self): return self.MS // 4
    @property
    def SROW(self): return self.WIN + 256
    @property
    def PASSES(self):
        out, t = [], self.TOK
        while t > 0:
            p = min(1024, t); out.append(p); t -= p
        return out


def chunks_of(n, c=512):
    out, o = [], 0
    while o < n:
        w = min(c, n - o); out.append((o, w)); o += w
    return out


class Sem:
    def __init__(self, es, nc, name):
        self.h = es.enter_context(nc.semaphore(name))
        self.v = 0
    def inc(self, inst, n=1):
        inst.then_inc(self.h, n); self.v += n; return self.v
    def wait(self, eng, v=None):
        eng.wait_ge(self.h, self.v if v is None else v)


def build(cfg: CFG):
    nc = bass.Bass("TRN2", target_bir_lowering=False)
    TOK, WIN, D, DC, HEADS, DQ, NB, E, M = (cfg.TOK, cfg.WIN, cfg.D, cfg.DC,
                                            cfg.HEADS, cfg.DQ, cfg.NB, cfg.E, cfg.M)
    QT, WT, MS, NQ, SROW = cfg.QT, cfg.WT, cfg.MS, cfg.NQ, cfg.SROW
    PASSES = cfg.PASSES
    NPAIR = HEADS // 2
    PMAX = max(PASSES)
    NTILE_MAX = PMAX // 128

    dt = nc.dram_tensor
    XFM = dt("XFM", [128, DC * TOK], F32, kind="ExternalInput").ap()
    WGQ = dt("WGQ", [E, MS // 2, 128, 8 * 256], F32, kind="ExternalInput").ap()
    WUQ = dt("WUQ", [E, MS // 2, 128, 8 * 256], F32, kind="ExternalInput").ap()
    WOQ = dt("WOQ", [E, NQ, 128, 4 * D], F32, kind="ExternalInput").ap()
    BGT = dt("BGT", [128, E * MS], F32, kind="ExternalInput").ap()
    BUT = dt("BUT", [128, E * MS], F32, kind="ExternalInput").ap()
    BO = dt("BO", [E, D], F32, kind="ExternalInput").ap()
    WRT = dt("WRT", [128, DC * E], F32, kind="ExternalInput").ap()
    WQL = dt("WQL", [NPAIR, 128, DC * 128], F32, kind="ExternalInput").ap()
    WKL = dt("WKL", [NPAIR, 128, DC * 128], F32, kind="ExternalInput").ap()
    WVL = dt("WVL", [NPAIR, 128, DC * 128], F32, kind="ExternalInput").ap()
    WPL = dt("WPL", [NPAIR, 128, DC * 128], F32, kind="ExternalInput").ap()
    WRL = dt("WRL", [2, 128, DC * 128], F32, kind="ExternalInput").ap()
    BQT = dt("BQT", [128, NPAIR], F32, kind="ExternalInput").ap()
    BRT = dt("BRT", [128, 2], F32, kind="ExternalInput").ap()
    BPJ = dt("BPJ", [1, D], F32, kind="ExternalInput").ap()
    BNDR = dt("BNDR", [NB, WIN], F32, kind="ExternalInput").ap()
    IDN = dt("IDN", [128, 128], F32, kind="ExternalInput").ap()
    CIOTA = dt("CIOTA", [128, 512], F32, kind="ExternalInput").ap()
    IDNF = dt("IDNF", [128, 128], F32, kind="ExternalInput").ap()
    CCOL = dt("CCOL", [128, 4], F32, kind="ExternalInput").ap()
    TRIU = dt("TRIU", [128, 128], F32, kind="ExternalInput").ap()
    BOF = dt("BOF", [1, E * D], F32, kind="ExternalInput").ap()
    OUT = dt("OUT", [128, DC * TOK], F32, kind="ExternalOutput").ap()
    RSCR = dt("RSCR", [256, TOK], BF16, kind="Internal").ap()
    ESCR = dt("ESCR", [HEADS, TOK * SROW], BF16, kind="Internal").ap()

    es = ExitStack()
    with es:
        EC = es.enter_context
        x_fm = EC(nc.sbuf_tensor("x_fm", [128, DC * TOK], F32))
        idn_bf = EC(nc.sbuf_tensor("idn_bf", [128, 128], BF16))
        ones1_f = EC(nc.sbuf_tensor("ones1_f", [1, 128], F32))
        onesK_f = EC(nc.sbuf_tensor("onesK_f", [128, 1], F32))
        ones_bf = EC(nc.sbuf_tensor("ones_bf", [128, 128], BF16))
        bgt = EC(nc.sbuf_tensor("bgt", [128, E * MS], F32))
        but = EC(nc.sbuf_tensor("but", [128, E * MS], F32))
        bo_sb = EC(nc.sbuf_tensor("bo_sb", [E, D], BF16))
        ciota = EC(nc.sbuf_tensor("ciota", [128, 512], F32))
        idn_f = EC(nc.sbuf_tensor("idn_f", [128, 128], F32))
        ccol = EC(nc.sbuf_tensor("ccol", [128, 4], F32))
        triu = EC(nc.sbuf_tensor("triu", [128, 128], F32))
        ones1_bf = EC(nc.sbuf_tensor("ones1_bf", [1, 128], BF16))
        wrt = EC(nc.sbuf_tensor("wrt", [128, DC * E], BF16))
        bqt = EC(nc.sbuf_tensor("bqt", [128, NPAIR], F32))
        brt = EC(nc.sbuf_tensor("brt", [128, 2], F32))
        stat = EC(nc.sbuf_tensor("stat", [1, 2 * PMAX], F32))
        SG = PMAX
        tmp32 = [EC(nc.sbuf_tensor(f"tmp32_{i}", [128, 512], F32)) for i in range(2)]
        sqt = tmp32
        wtok = EC(nc.sbuf_tensor("wtok", [128, 8 * NTILE_MAX], F32))
        rt_sb = EC(nc.sbuf_tensor("rt_sb", [128, 2 * E], F32))
        rvec1 = EC(nc.sbuf_tensor("rvec1", [128, 8], F32))
        if cfg.SPARSE:
            whalf = wtok_bf = None
            wtok_bfa = EC(nc.sbuf_tensor("wtok_bfa", [128, 8 * NTILE_MAX], BF16))
        else:
            whalf = EC(nc.sbuf_tensor("whalf", [E, PMAX], BF16))
            wtok_bf = EC(nc.sbuf_tensor("wtok_bf", [128, E], BF16))
            wtok_bfa = None

        SP3 = EC(nc.psum_tensor("SP3", [128, 1536], F32))
        PA = EC(nc.psum_tensor("PA", [128, 512], F32))
        PBC = EC(nc.psum_tensor("PBC", [128, 1024], F32))
        PTQ = [EC(nc.psum_tensor(f"PTQ{i}", [128, 512], F32)) for i in range(2)]

        class PV:
            def __init__(self, t, off):
                self.t, self.off = t, off
            def __getitem__(self, idx):
                sp, sf = idx
                o0 = self.off + (sf.start or 0)
                o1 = self.off + sf.stop
                return self.t[sp, o0:o1]

        P = [PV(SP3, 0), PV(SP3, 512), PV(SP3, 1024), PV(PA, 0), PV(PBC, 0), PV(PBC, 512)]
        SPAN = (WT + 1) * 128

        dX = Sem(es, nc, "dX")
        dXg = Sem(es, nc, "dXg")
        dW2 = [Sem(es, nc, "dW0"), Sem(es, nc, "dW1")]
        dWO = Sem(es, nc, "dWO")
        dWOb = Sem(es, nc, "dWOb")
        dBF = Sem(es, nc, "dBF")
        dL = Sem(es, nc, "dL")
        dR2 = [Sem(es, nc, "dR0"), Sem(es, nc, "dR1")]
        dG2 = [Sem(es, nc, "dG0"), Sem(es, nc, "dG1")]
        dE2 = [Sem(es, nc, "dE0"), Sem(es, nc, "dE1")]
        dS = Sem(es, nc, "dS")
        dO = Sem(es, nc, "dO")
        T = Sem(es, nc, "T")
        V = Sem(es, nc, "V")
        S = Sem(es, nc, "S")
        WGF = Sem(es, nc, "WGF")
        WOF = Sem(es, nc, "WOF")

        block = EC(nc.Block())
        st = {}

        def pguard(eng, ps):
            g = st.get(("pg", id(ps)))
            if g is not None:
                sem, v = g
                sem.wait(eng, v)
        def pset(ps, sem, v=None):
            st[("pg", id(ps))] = (sem, sem.v if v is None else v)

        def sec(name):
            return getattr(block, name)

        def scope_in(name):
            sid, _ = nc.enter_named_scope(name, False)
            return (name, sid)
        def scope_out(tok):
            nc.leave_named_scope(tok[0], tok[1], False)

        def vdep(v, inst):
            V.inc(inst); V.wait(v)
        def sdep(s, inst):
            S.inc(inst); S.wait(s)

        @sec("sync")
        def _(sy):
            sy.dma_start(out=x_fm[:], in_=XFM).then_inc(dX.h, 16)
            sy.dma_start(out=bgt[:], in_=BGT).then_inc(dX.h, 16)
            sy.dma_start(out=but[:], in_=BUT).then_inc(dX.h, 16)
            sy.dma_start(out=bqt[:], in_=BQT).then_inc(dX.h, 16)
            sy.dma_start(out=brt[:], in_=BRT).then_inc(dX.h, 16)
            sy.dma_start(out=ciota[:], in_=CIOTA).then_inc(dX.h, 16)
            sy.dma_start(out=ccol[:], in_=CCOL).then_inc(dX.h, 16)
        dX.v += 112
        @sec("gpsimd")
        def _(g):
            g.dma_start(out=idn_bf[:], in_=IDN).then_inc(dXg.h, 16)
            g.dma_start(out=bo_sb[:], in_=BO).then_inc(dXg.h, 16)
            g.dma_start(out=wrt[:], in_=WRT).then_inc(dXg.h, 16)
            g.dma_start(out=triu[:], in_=TRIU).then_inc(dXg.h, 16)
            g.dma_start(out=idn_f[:], in_=IDNF).then_inc(dXg.h, 16)
        dXg.v += 80
        X_READY = dX.v
        XG_READY = dXg.v
        @sec("vector")
        def _(v):
            v.memset(ones1_f[:], 1.0)
            v.memset(onesK_f[:], 1.0)
            v.memset(ones_bf[:], 1.0)
            inst = v.memset(ones1_bf[:], 1.0)
            V.inc(inst)
        CONST_READY = V.v

        def phase_join_waits():
            return [(T, T.v), (V, V.v), (S, S.v), (dW2[0], dW2[0].v), (dW2[1], dW2[1].v), (dWO, dWO.v), (dWOb, dWOb.v), (dBF, dBF.v), (dL, dL.v), (dR2[0], dR2[0].v), (dR2[1], dR2[1].v), (dG2[0], dG2[0].v), (dG2[1], dG2[1].v), (dE2[0], dE2[0].v), (dE2[1], dE2[1].v), (dS, dS.v)]

        def wait_join(eng, join):
            for sem, v in join:
                if v > 0:
                    sem.wait(eng, v)

        def emit_ffn(tag, join):
            with ExitStack() as fes:
                FE = fes.enter_context
                NH = MS // 2
                wgq_sb = [FE(nc.sbuf_tensor(f"{tag}_wg{i}", [128, 8 * 256], BF16)) for i in range(2)]
                wuq_sb = [FE(nc.sbuf_tensor(f"{tag}_wu{i}", [128, 8 * 256], BF16)) for i in range(2)]
                SP_COLS = max(DC * PMAX, (NTILE_MAX + DC) * 512) if cfg.SPARSE else DC * PMAX
                xt_sb = FE(nc.sbuf_tensor(f"{tag}_xt", [128, SP_COLS], BF16))
                if cfg.SPARSE:
                    wo_sb = FE(nc.sbuf_tensor(f"{tag}_wo", [128, min(NQ, 2) * 4 * D], BF16))
                    HWS = MS
                    hw_sb = FE(nc.sbuf_tensor(f"{tag}_hw", [128, HWS * 512], BF16))
                    eo_sb = FE(nc.sbuf_tensor(f"{tag}_eo", [128, 4 * 1024], BF16))
                    xtk = FE(nc.sbuf_tensor(f"{tag}_xtk", [128, (PMAX // 128) * D], BF16))
                    sg_sb = FE(nc.sbuf_tensor(f"{tag}_sg", [128, 512], BF16))
                    pmt_sb = FE(nc.sbuf_tensor(f"{tag}_pmt", [128, 4 * 512], BF16))
                    wcol = FE(nc.sbuf_tensor(f"{tag}_wcol", [128, 8], F32))
                    s_all = FE(nc.sbuf_tensor(f"{tag}_sall", [128, 8 * (PMAX // 128)], F32))
                    bases = FE(nc.sbuf_tensor(f"{tag}_bases", [1, 8 * (PMAX // 128 + 1)], F32))
                    m_bf = FE(nc.sbuf_tensor(f"{tag}_mbf", [128, 8], F32))
                    bof_tmp = FE(nc.sbuf_tensor(f"{tag}_bof", [1, D], BF16))
                    wb_sb = None
                else:
                    wo_sb = FE(nc.sbuf_tensor(f"{tag}_wo", [128, NQ * 4 * D], BF16))
                    hw_sb = FE(nc.sbuf_tensor(f"{tag}_hw", [128, (MS // 2) * PMAX], BF16))
                    wb_sb = FE(nc.sbuf_tensor(f"{tag}_wb", [128, PMAX], BF16))

                quarters = [(pi, e, h) for pi in range(len(PASSES)) for e in range(E) for h in range(NH)]
                def emit_gu_dma(k):
                    if k >= len(quarters):
                        return
                    pi, e, q = quarters[k]
                    buf = k % 2
                    dwp = dW2[buf]
                    @sec("gpsimd")
                    def _(g, e=e, q=q, buf=buf, k=k, dwp=dwp):
                        if k < 2:
                            wait_join(g, join)
                        wf = st.get(("wgf_at", k - 2))
                        if wf is not None:
                            WGF.wait(g, wf)
                        g.dma_start(out=wgq_sb[buf][:], in_=WGQ[e, q]).then_inc(dwp.h, 16)
                        g.dma_start(out=wuq_sb[buf][:], in_=WUQ[e, q]).then_inc(dwp.h, 16)
                    dwp.v += 32
                    st[("gu_ld", k)] = (dwp, dwp.v)
                emit_gu_dma(0)
                emit_gu_dma(1)

                kq = 0
                pass_off = 0
                for p_idx, ptok in enumerate(PASSES):
                    pchunks = chunks_of(ptok)
                    ntile = ptok // 128

                    _sc = scope_in(f"{tag}.lnrt{p_idx}")
                    for ci, (co, cw) in enumerate(pchunks):
                        xoff = pass_off + co
                        for dc in range(DC):
                            @sec("scalar")
                            def _(s, ci=ci, dc=dc, xoff=xoff, cw=cw):
                                if ci == 0 and dc == 0:
                                    dX.wait(s, X_READY)
                                    dXg.wait(s, XG_READY)
                                    wait_join(s, join)
                                k = ci * DC + dc
                                wk = st.get((tag, p_idx, "sqfree", k - 2))
                                if wk is not None:
                                    T.wait(s, wk)
                                if dc == 0:
                                    V.wait(s)
                                inst = s.activation(out=sqt[dc % 2][:, 0:cw],
                                                    in_=x_fm[:, dc * TOK + xoff: dc * TOK + xoff + cw],
                                                    func=AF.Square)
                                S.inc(inst)
                                st[(tag, p_idx, "sq", ci, dc)] = S.v
                            @sec("tensor")
                            def _(t, ci=ci, dc=dc, xoff=xoff, cw=cw):
                                if ci == 0 and dc == 0:
                                    dX.wait(t, X_READY)
                                    dXg.wait(t, XG_READY)
                                    V.wait(t, CONST_READY)
                                    wait_join(t, join)
                                    pguard(t, P[4]); pguard(t, P[5])
                                t.matmul(P[4][0:1, 0:cw], onesK_f[:, :],
                                         x_fm[:, dc * TOK + xoff: dc * TOK + xoff + cw],
                                         start=(dc == 0), stop=(dc == DC - 1))
                                S.wait(t, st[(tag, p_idx, "sq", ci, dc)])
                                inst = t.matmul(P[5][0:1, 0:cw], onesK_f[:, :], sqt[dc % 2][:, 0:cw],
                                                start=(dc == 0), stop=(dc == DC - 1))
                                T.inc(inst)
                                st[(tag, p_idx, "sqfree", ci * DC + dc)] = T.v
                        st[(tag, p_idx, "stats", ci)] = T.v
                        @sec("vector")
                        def _(v, ci=ci, co=co, cw=cw):
                            T.wait(v, st[(tag, p_idx, "stats", ci)])
                            V.wait(v)
                            vdep(v, v.tensor_scalar(out=stat[0:1, co:co + cw], in0=P[4][0:1, 0:cw],
                                            scalar1=1.0 / D, scalar2=None, op0=ALU.mult))
                            vdep(v, v.tensor_scalar(out=stat[0:1, SG + co:SG + co + cw], in0=P[5][0:1, 0:cw],
                                            scalar1=1.0 / D, scalar2=None, op0=ALU.mult))
                            vdep(v, v.tensor_tensor(out=tmp32[0][0:1, 0:cw], in0=stat[0:1, co:co + cw],
                                            in1=stat[0:1, co:co + cw], op=ALU.mult))
                            vdep(v, v.tensor_tensor(out=stat[0:1, SG + co:SG + co + cw], in0=stat[0:1, SG + co:SG + co + cw],
                                            in1=tmp32[0][0:1, 0:cw], op=ALU.subtract))
                            inst = v.tensor_scalar(out=stat[0:1, SG + co:SG + co + cw], in0=stat[0:1, SG + co:SG + co + cw],
                                                   scalar1=cfg.EPS, scalar2=None, op0=ALU.add)
                            V.inc(inst)
                            st[(tag, p_idx, "var", ci)] = V.v
                            pset(P[4], V); pset(P[5], V)
                        if ci + 1 < len(pchunks):
                            @sec("tensor")
                            def _(t, ci=ci):
                                V.wait(t, st[(tag, p_idx, "var", ci)])
                    @sec("scalar")
                    def _(s):
                        V.wait(s, st[(tag, p_idx, "var", len(pchunks) - 1)])
                        inst = s.activation(out=stat[0:1, SG:SG + ptok], in_=stat[0:1, SG:SG + ptok], func=AF.Sqrt)
                        S.inc(inst)
                    SQ_D = S.v
                    @sec("vector")
                    def _(v):
                        S.wait(v, SQ_D)
                        inst = v.reciprocal(stat[0:1, SG:SG + ptok], stat[0:1, SG:SG + ptok])
                        V.inc(inst)
                    RSTD_D = V.v

                    for ci, (co, cw) in enumerate(pchunks):
                        xoff = pass_off + co
                        @sec("tensor")
                        def _(t, ci=ci, co=co, cw=cw):
                            if ci == 0:
                                V.wait(t, RSTD_D)
                            pguard(t, P[0]); pguard(t, P[1])
                            t.matmul(P[0][:, 0:cw], ones1_f[:, :], stat[0:1, co:co + cw],
                                     start=True, stop=True)
                            inst = t.matmul(P[1][:, 0:cw], ones1_f[:, :], stat[0:1, SG + co:SG + co + cw],
                                            start=True, stop=True)
                            T.inc(inst)
                            st[(tag, p_idx, "bcast", ci)] = T.v
                        @sec("vector")
                        def _(v, ci=ci, co=co, cw=cw, xoff=xoff):
                            T.wait(v, st[(tag, p_idx, "bcast", ci)])
                            V.wait(v)
                            for dc in range(DC):
                                vdep(v, v.tensor_tensor(out=tmp32[dc % 2][:, 0:cw],
                                                in0=x_fm[:, dc * TOK + xoff: dc * TOK + xoff + cw],
                                                in1=P[0][:, 0:cw], op=ALU.subtract))
                                inst = v.tensor_tensor(out=xt_sb[:, dc * PMAX + co: dc * PMAX + co + cw],
                                                       in0=tmp32[dc % 2][:, 0:cw],
                                                       in1=P[1][:, 0:cw], op=ALU.mult)
                                vdep(v, inst)
                            pset(P[0], V); pset(P[1], V)
                    XT_READY = V.v

                    for ti in range(ntile):
                        toff = ti * 128
                        @sec("tensor")
                        def _(t, ti=ti, toff=toff):
                            if ti == 0:
                                V.wait(t, XT_READY)
                            pguard(t, P[2])
                            for dc in range(DC):
                                inst = t.matmul(P[2][0:128, 0:E],
                                                xt_sb[:, dc * PMAX + toff: dc * PMAX + toff + 128],
                                                wrt[:, dc * E:(dc + 1) * E],
                                                start=(dc == 0), stop=(dc == DC - 1))
                            T.inc(inst)
                            st[("rt", ti)] = T.v
                        @sec("vector")
                        def _(v, ti=ti):
                            T.wait(v, st[("rt", ti)])
                            inst = v.reduce_max(rvec1[:, 0:1], P[2][0:128, 0:E], axis=AX.X, negate=True)
                            V.inc(inst)
                            st[("rtmax", ti)] = V.v
                        @sec("scalar")
                        def _(s, ti=ti):
                            V.wait(s, st[("rtmax", ti)])
                            inst = s.activation(out=rt_sb[:, 0:E], in_=P[2][0:128, 0:E], func=AF.Exp,
                                                bias=rvec1[:, 0:1], accum_out=rvec1[:, 1:2])
                            S.inc(inst)
                            st[("rtexp", ti)] = S.v
                        @sec("vector")
                        def _(v, ti=ti):
                            S.wait(v, st[("rtexp", ti)])
                            V.wait(v)
                            pset(P[2], S, st[("rtexp", ti)])
                            vdep(v, v.reciprocal(rvec1[:, 2:3], rvec1[:, 1:2]))
                            vdep(v, v.tensor_scalar(out=rt_sb[:, 0:E], in0=rt_sb[:, 0:E],
                                            scalar1=rvec1[:, 2:3], scalar2=None, op0=ALU.mult))
                            vdep(v, v.reduce_max(rvec1[:, 3:4], rt_sb[:, 0:E], axis=AX.X))
                            vdep(v, v.tensor_scalar(out=rt_sb[:, E:2 * E], in0=rt_sb[:, 0:E],
                                            scalar1=rvec1[:, 3:4], scalar2=None, op0=ALU.is_lt))
                            vdep(v, v.tensor_tensor(out=rt_sb[:, E:2 * E], in0=rt_sb[:, 0:E],
                                            in1=rt_sb[:, E:2 * E], op=ALU.mult))
                            vdep(v, v.reduce_max(rvec1[:, 4:5], rt_sb[:, E:2 * E], axis=AX.X))
                            vdep(v, v.tensor_tensor(out=rvec1[:, 5:6], in0=rvec1[:, 3:4],
                                            in1=rvec1[:, 4:5], op=ALU.add))
                            vdep(v, v.tensor_scalar(out=rvec1[:, 5:6], in0=rvec1[:, 5:6],
                                            scalar1=1e-8, scalar2=None, op0=ALU.add))
                            vdep(v, v.reciprocal(rvec1[:, 6:7], rvec1[:, 5:6]))
                            vdep(v, v.tensor_scalar(out=rt_sb[:, E:2 * E], in0=rt_sb[:, 0:E],
                                            scalar1=rvec1[:, 4:5], scalar2=None, op0=ALU.is_ge))
                            vdep(v, v.tensor_tensor(out=rt_sb[:, E:2 * E], in0=rt_sb[:, 0:E],
                                            in1=rt_sb[:, E:2 * E], op=ALU.mult))
                            inst = v.tensor_scalar(out=wtok[:, ti * 8:ti * 8 + E], in0=rt_sb[:, E:2 * E],
                                                   scalar1=rvec1[:, 6:7], scalar2=0.5,
                                                   op0=ALU.mult, op1=ALU.mult)
                            V.inc(inst)
                            st[("wtok", ti)] = V.v
                        if cfg.SPARSE:
                            @sec("vector")
                            def _(v, ti=ti):
                                V.wait(v)
                                inst = v.tensor_copy(wtok_bfa[:, ti * 8:ti * 8 + E],
                                                     wtok[:, ti * 8:ti * 8 + E])
                                V.inc(inst)
                        else:
                            @sec("vector")
                            def _(v, ti=ti):
                                if ti > 0:
                                    T.wait(v, st[("whalftp", ti - 1)])
                                V.wait(v)
                                inst = v.tensor_copy(wtok_bf[:], wtok[:, ti * 8:ti * 8 + E])
                                V.inc(inst)
                                st[("wtokbf", ti)] = V.v
                            @sec("tensor")
                            def _(t, ti=ti):
                                V.wait(t, st[("wtokbf", ti)])
                                pguard(t, PTQ[0])
                                inst = t.matmul(PTQ[0][0:E, 0:128], wtok_bf[:], idn_bf[:],
                                                start=True, stop=True)
                                T.inc(inst)
                                st[("whalftp", ti)] = T.v
                            @sec("scalar")
                            def _(s, ti=ti, toff=toff):
                                T.wait(s, st[("whalftp", ti)])
                                inst = s.copy(out=whalf[0:E, toff:toff + 128], in_=PTQ[0][0:E, 0:128])
                                S.inc(inst)
                                pset(PTQ[0], S)
                    WTOK_ALL = st[("wtok", ntile - 1)]
                    WHALF_ALL = S.v

                    if cfg.SPARSE:
                        pexp = 2.0 / E
                        capf = int(2 * ptok / E + 8.0 * (ptok * pexp * (1 - pexp)) ** 0.5)
                        CAP = min(512, -(-capf // 128) * 128)
                        if tag != "ffn1":
                            CAP = min(512, max(CAP, -(-(ptok // 2) // 128) * 128))
                        NCC = CAP // 128
                        ngrp = 2 * NCC
                        XG_OFF = ntile * CAP
                        assert XG_OFF + DC * CAP <= SP_COLS and NCC * 1024 <= SP_COLS

                        def GOUT(grp, w=512):
                            return P[grp][0:128, 0:w] if grp < 6 else PTQ[grp - 6][:, 0:w]

                        def GKEY(grp):
                            return P[grp] if grp < 6 else PTQ[grp - 6]

                        @sec("vector")
                        def _(v):
                            V.wait(v)
                            inst = v.memset(bases[0:1, 0:E], 0.0)
                            V.inc(inst)
                        st["bs0"] = V.v
                        for ti in range(ntile):
                            @sec("vector")
                            def _(v, ti=ti):
                                V.wait(v, st[("wtok", ti)])
                                wk = st.get(("sp_cnt", ti - 1))
                                if wk is not None:
                                    T.wait(v, wk)
                                V.wait(v)
                                inst = v.tensor_scalar(out=m_bf[:, 0:E], in0=wtok[:, ti * 8:ti * 8 + E],
                                                       scalar1=0.0, scalar2=None, op0=ALU.is_gt)
                                V.inc(inst)
                                st[("sp_m", ti)] = V.v
                            @sec("tensor")
                            def _(t, ti=ti):
                                V.wait(t, st[("sp_m", ti)])
                                pguard(t, P[4])
                                inst = t.matmul(P[4][0:128, 0:E], triu[:, :], m_bf[:, 0:E],
                                                start=True, stop=True)
                                T.inc(inst)
                                pguard(t, P[5])
                                inst = t.matmul(P[5][0:1, 0:E], onesK_f[:, :], m_bf[:, 0:E],
                                                start=True, stop=True)
                                T.inc(inst)
                                st[("sp_cnt", ti)] = T.v
                                pguard(t, P[3])
                                V.wait(t, st.get(("sp_bok", ti - 1), st["bs0"]))
                                inst = t.matmul(P[3][0:128, 0:E], ones1_f[:, :],
                                                bases[0:1, ti * 8:ti * 8 + E], start=True, stop=True)
                                T.inc(inst)
                                st[("sp_bb", ti)] = T.v
                            @sec("vector")
                            def _(v, ti=ti):
                                T.wait(v, st[("sp_bb", ti)])
                                V.wait(v)
                                vdep(v, v.tensor_copy(s_all[:, ti * 8:ti * 8 + E], P[4][0:128, 0:E]))
                                vdep(v, v.tensor_tensor(out=s_all[:, ti * 8:ti * 8 + E],
                                                        in0=s_all[:, ti * 8:ti * 8 + E],
                                                        in1=P[3][0:128, 0:E], op=ALU.add))
                                vdep(v, v.tensor_scalar(out=s_all[:, ti * 8:ti * 8 + E],
                                                        in0=s_all[:, ti * 8:ti * 8 + E],
                                                        scalar1=1e6, scalar2=None, op0=ALU.add))
                                vdep(v, v.scalar_tensor_tensor(out=s_all[:, ti * 8:ti * 8 + E],
                                                               in0=m_bf[:, 0:E], scalar=-1e6,
                                                               in1=s_all[:, ti * 8:ti * 8 + E],
                                                               op0=ALU.mult, op1=ALU.add))
                                vdep(v, v.tensor_tensor(out=bases[0:1, (ti + 1) * 8:(ti + 1) * 8 + E],
                                                        in0=bases[0:1, ti * 8:ti * 8 + E],
                                                        in1=P[5][0:1, 0:E], op=ALU.add))
                                pset(P[4], V); pset(P[5], V); pset(P[3], V)
                                st[("sp_bok", ti)] = V.v
                        SALL_RDY = st[("sp_bok", ntile - 1)]
                        for ti in range(ntile):
                            for gg in range(2):
                                bank = PTQ[gg]
                                @sec("tensor")
                                def _(t, ti=ti, gg=gg, bank=bank):
                                    if ti == 0 and gg == 0:
                                        V.wait(t, XT_READY)
                                    pguard(t, bank)
                                    for dd in range(4):
                                        dc = gg * 4 + dd
                                        inst = t.matmul(bank[:, dd * 128:(dd + 1) * 128],
                                                        xt_sb[:, dc * PMAX + ti * 128: dc * PMAX + ti * 128 + 128],
                                                        idn_bf[:], start=True, stop=True)
                                    T.inc(inst)
                                    st[("xtk_t", ti, gg)] = T.v
                                @sec("vector")
                                def _(v, ti=ti, gg=gg, bank=bank):
                                    T.wait(v, st[("xtk_t", ti, gg)])
                                    inst = v.tensor_copy(xtk[:, ti * D + gg * 512: ti * D + gg * 512 + 512],
                                                         bank[:, 0:512])
                                    V.inc(inst)
                                    pset(bank, V)
                        XTK_RDY = V.v

                        pm_all = xt_sb
                        EVORD = [g for g in (4, 5, 3) if g < ngrp] + \
                                [g for g in range(ngrp) if g not in (4, 5, 3)]

                        def emit_tail(e, wc_v):
                            wsl = (e % 2) * 4
                            @sec("scalar")
                            def _(s, e=e, wsl=wsl, wc_v=wc_v):
                                for g_i, grp in enumerate(EVORD):
                                    cc, dh = divmod(grp, 2)
                                    osem, oval = st[("sp_op", grp)]
                                    osem.wait(s, oval)
                                    if g_i == 0:
                                        V.wait(s, wc_v)
                                    inst = s.activation(out=eo_sb[:, cc * 1024 + dh * 512: cc * 1024 + dh * 512 + 512],
                                                        in_=GOUT(grp), func=AF.Copy,
                                                        scale=wcol[:, wsl + cc:wsl + cc + 1])
                                    S.inc(inst)
                                    pset(GKEY(grp), S)
                            EO_RDY = S.v
                            st[("eo_rdy_slot", e % 2)] = S.v
                            @sec("tensor")
                            def _(t, e=e):
                                pguard(t, P[4]); pguard(t, P[5])
                                V.wait(t, st[("sp_xg", DC - 1)])
                                for ti in range(ntile):
                                    dst = (P[4][0:1, ti * 128: ti * 128 + 128] if ti < 4 else
                                           P[5][0:1, (ti - 4) * 128: (ti - 4) * 128 + 128])
                                    inst = t.matmul(dst, s_all[:, ti * 8 + e: ti * 8 + e + 1],
                                                    idn_f[:, :], start=True, stop=True)
                                T.inc(inst)
                                st["sp_sr_t"] = T.v
                            @sec("vector")
                            def _(v, e=e, ptok=ptok):
                                T.wait(v, st["sp_sr_t"])
                                V.wait(v)
                                inst = v.tensor_copy(stat[0:1, 0:min(512, ptok)],
                                                     P[4][0:1, 0:min(512, ptok)])
                                V.inc(inst)
                                if ptok > 512:
                                    inst = v.tensor_copy(stat[0:1, 512:ptok], P[5][0:1, 0:ptok - 512])
                                    V.inc(inst)
                                pset(P[4], V); pset(P[5], V)
                                st["sp_sr"] = V.v
                            for td, (co, cw) in enumerate(chunks_of(ptok)):
                                @sec("tensor")
                                def _(t, e=e, td=td, co=co, cw=cw):
                                    V.wait(t, st["sp_sr"])
                                    pguard(t, PTQ[0])
                                    inst = t.matmul(PTQ[0][:, 0:cw], ones1_f[:, :],
                                                    stat[0:1, co:co + cw], start=True, stop=True)
                                    T.inc(inst)
                                    st[("sp_br", td)] = T.v
                                @sec("vector")
                                def _(v, e=e, td=td, cw=cw):
                                    T.wait(v, st[("sp_br", td)])
                                    wk = st.get("sp_pmt_free")
                                    if wk is not None:
                                        T.wait(v, wk)
                                    V.wait(v)
                                    for cc in range(NCC):
                                        inst = v.tensor_scalar(out=pmt_sb[:, cc * 512:cc * 512 + cw],
                                                               in0=PTQ[0][:, 0:cw],
                                                               scalar1=ccol[:, cc:cc + 1],
                                                               scalar2=None, op0=ALU.is_equal)
                                        V.inc(inst)
                                    pset(PTQ[0], V)
                                    st[("sp_pmt", td)] = V.v
                                for dc in range(DC):
                                    bank = P[dc % 4]
                                    @sec("tensor")
                                    def _(t, e=e, td=td, dc=dc, bank=bank, cw=cw, EO_RDY=EO_RDY):
                                        V.wait(t, st[("sp_pmt", td)])
                                        if dc == 0:
                                            S.wait(t, EO_RDY)
                                        wk = st.get(("sp_ev", dc - 4))
                                        if wk is not None:
                                            V.wait(t, wk)
                                        pguard(t, bank)
                                        for cc in range(NCC):
                                            inst = t.matmul(bank[:, 0:cw],
                                                            eo_sb[:, cc * 1024 + dc * 128: cc * 1024 + dc * 128 + 128],
                                                            pmt_sb[:, cc * 512:cc * 512 + cw],
                                                            start=(cc == 0), stop=(cc == NCC - 1))
                                        T.inc(inst)
                                        st[("sp_sc", td, dc)] = T.v
                                        if dc == DC - 1:
                                            st["sp_pmt_free"] = T.v
                                    @sec("vector")
                                    def _(v, e=e, td=td, dc=dc, bank=bank, co=co, cw=cw, pass_off=pass_off):
                                        T.wait(v, st[("sp_sc", td, dc)])
                                        V.wait(v)
                                        xoff = pass_off + co
                                        inst = v.tensor_tensor(out=x_fm[:, dc * TOK + xoff: dc * TOK + xoff + cw],
                                                               in0=x_fm[:, dc * TOK + xoff: dc * TOK + xoff + cw],
                                                               in1=bank[:, 0:cw], op=ALU.add)
                                        V.inc(inst)
                                        pset(bank, V)
                                        st[("sp_ev", dc)] = V.v
                                for dc in range(DC):
                                    st.pop(("sp_ev", dc), None)
                            st[("gcons", id(P[0]))] = (V, V.v)
                            st[("gcons", id(P[1]))] = (V, V.v)

                        pending = None
                        for e in range(E):
                            eg = kq // NH
                            wsl = (e % 2) * 4

                            def emit_wo_dma(q, e=e, eg=eg):
                                buf = q % 2
                                dsem = dWO if buf == 0 else dWOb
                                @sec("gpsimd")
                                def _(g, q=q, buf=buf, e=e, eg=eg):
                                    wk = None
                                    if q >= 2:
                                        wk = st.get((tag, "wo_at", eg, q - 2))
                                    else:
                                        cand = [c for c in range(NQ) if c % 2 == buf]
                                        if cand:
                                            wk = st.get((tag, "wo_at", eg - 1, cand[-1]))
                                    if wk is not None:
                                        wk[0].wait(g, wk[1])
                                    g.dma_start(out=wo_sb[:, buf * 4 * D:(buf + 1) * 4 * D],
                                                in_=WOQ[e, q]).then_inc(dsem.h, 16)
                                dsem.v += 16
                                st[(tag, "wo_ld", eg, q)] = (dsem, dsem.v)
                            emit_wo_dma(0)
                            if NQ > 1:
                                emit_wo_dma(1)

                            @sec("vector")
                            def _(v, e=e):
                                V.wait(v, SALL_RDY)
                                V.wait(v, XTK_RDY)
                                if e == 0:
                                    T.wait(v)
                                else:
                                    T.wait(v, st["sp_wc_t"])
                                V.wait(v)
                                for ti in range(ntile):
                                    inst = v.tensor_scalar(out=pm_all[:, ti * CAP: (ti + 1) * CAP],
                                                           in0=ciota[:, 0:CAP],
                                                           scalar1=s_all[:, ti * 8 + e: ti * 8 + e + 1],
                                                           scalar2=None, op0=ALU.is_equal)
                                    V.inc(inst)
                                st[("sp_pm", e)] = V.v
                            eo_base = S.v if pending is not None else None
                            for dc in range(DC):
                                bank = P[4] if dc % 2 == 0 else P[5]
                                grp_b = 4 if dc % 2 == 0 else 5
                                @sec("tensor")
                                def _(t, e=e, dc=dc, bank=bank, grp_b=grp_b, eo_base=eo_base):
                                    if dc == 0:
                                        V.wait(t, st[("sp_pm", e)])
                                    if dc < 2 and eo_base is not None and grp_b < ngrp:
                                        S.wait(t, eo_base + EVORD.index(grp_b) + 1)
                                    pguard(t, bank)
                                    for ti in range(ntile):
                                        inst = t.matmul(bank[:, 0:CAP],
                                                        xtk[:, ti * D + dc * 128: ti * D + dc * 128 + 128],
                                                        pm_all[:, ti * CAP:(ti + 1) * CAP],
                                                        start=(ti == 0), stop=(ti == ntile - 1))
                                    T.inc(inst)
                                    st[("sp_ga", dc)] = T.v
                                @sec("vector")
                                def _(v, e=e, dc=dc, bank=bank):
                                    T.wait(v, st[("sp_ga", dc)])
                                    inst = v.tensor_copy(xt_sb[:, XG_OFF + dc * CAP: XG_OFF + (dc + 1) * CAP],
                                                         bank[:, 0:CAP])
                                    V.inc(inst)
                                    pset(bank, V)
                                    st[("sp_xg", dc)] = V.v
                            @sec("tensor")
                            def _(t, e=e, eo_base=eo_base):
                                if eo_base is not None:
                                    S.wait(t, eo_base + EVORD.index(3) + 1)
                                pguard(t, P[3])
                                for cc in range(NCC):
                                    for ti in range(ntile):
                                        inst = t.matmul(P[3][0:128, cc:cc + 1],
                                                        pm_all[:, ti * CAP + cc * 128: ti * CAP + cc * 128 + 128],
                                                        wtok_bfa[:, ti * 8 + e: ti * 8 + e + 1],
                                                        start=(ti == 0), stop=(ti == ntile - 1))
                                T.inc(inst)
                                st["sp_wc_t"] = T.v
                            @sec("vector")
                            def _(v, e=e, wsl=wsl):
                                T.wait(v, st["sp_wc_t"])
                                wk = st.get(("eo_rdy_slot", e % 2))
                                if wk is not None:
                                    S.wait(v, wk)
                                V.wait(v)
                                inst = v.tensor_copy(wcol[:, wsl:wsl + NCC], P[3][0:128, 0:NCC])
                                V.inc(inst)
                                pset(P[3], V)
                                st["sp_wc"] = V.v

                            if pending is not None:
                                emit_tail(*pending)
                            pending = (e, st["sp_wc"])

                            @sec("gpsimd")
                            def _(g, e=e):
                                wk = st.get("sp_bof_used")
                                if wk is not None:
                                    T.wait(g, wk)
                                g.dma_start(out=bof_tmp[:], in_=BOF[0:1, e * D:(e + 1) * D]).then_inc(dBF.h, 16)
                            dBF.v += 16
                            BOF_RD = dBF.v

                            s0 = S.v
                            v0 = V.v
                            sg_pred = {m: s0 + m + 1 for m in range(MS)}
                            hw_pred = {m: v0 + m + 1 for m in range(MS)}

                            def emit_oproj_chunk(q, e=e, eg=eg, BOF_RD=BOF_RD):
                                @sec("tensor")
                                def _(t, q=q, e=e, eg=eg):
                                    ldsem, ldval = st[(tag, "wo_ld", eg, q)]
                                    ldsem.wait(t, ldval)
                                    if q == 0:
                                        V.wait(t, hw_pred[MS - 1])
                                        dBF.wait(t, BOF_RD)
                                    buf = q % 2
                                    inst = None
                                    for grp in range(ngrp):
                                        cc, dh = divmod(grp, 2)
                                        if q == 0:
                                            pguard(t, GKEY(grp))
                                            t.matmul(GOUT(grp), ones1_bf[:, :],
                                                     bof_tmp[0:1, dh * 512: dh * 512 + 512],
                                                     start=True, stop=False)
                                        for mt in range(4):
                                            m = q * 4 + mt
                                            inst = t.matmul(GOUT(grp),
                                                            hw_sb[:, (m % HWS) * 512 + cc * 128: (m % HWS) * 512 + cc * 128 + 128],
                                                            wo_sb[:, buf * 4 * D + mt * D + dh * 512: buf * 4 * D + mt * D + dh * 512 + 512],
                                                            start=False, stop=(q == NQ - 1 and mt == 3))
                                        if q == NQ - 1:
                                            if grp == ngrp - 1:
                                                WOF.inc(inst)
                                                st["wof_last"] = WOF.v
                                                st[("sp_op", grp)] = (WOF, WOF.v)
                                            else:
                                                T.inc(inst)
                                                st[("sp_op", grp)] = (T, T.v)
                                    if q < NQ - 1:
                                        T.inc(inst)
                                        st[("sp_ch", q)] = T.v
                                        st[(tag, "wo_at", eg, q)] = (T, T.v)
                                if q + 2 < NQ:
                                    emit_wo_dma(q + 2)

                            for m in range(MS):
                                q2, msq = divmod(m, 2)
                                k = kq + q2
                                buf = k % 2
                                @sec("tensor")
                                def _(t, m=m, msq=msq, k=k, buf=buf, e=e):
                                    ldsem, ldval = st[("gu_ld", k)]
                                    ldsem.wait(t, ldval)
                                    if m == 0:
                                        V.wait(t, st[("sp_xg", DC - 1)])
                                    gw = st.get(("gcons", id(P[0])))
                                    if gw is not None:
                                        gw[0].wait(t, gw[1])
                                    pguard(t, P[0])
                                    for dc in range(DC):
                                        t.matmul(P[0][:, 0:CAP],
                                                 wgq_sb[buf][:, dc * 256 + msq * 128: dc * 256 + msq * 128 + 128],
                                                 xt_sb[:, XG_OFF + dc * CAP: XG_OFF + dc * CAP + CAP],
                                                 start=(dc == 0), stop=(dc == DC - 1))
                                    uw = st.get(("gcons", id(P[1])))
                                    if uw is not None:
                                        uw[0].wait(t, uw[1])
                                    pguard(t, P[1])
                                    for dc in range(DC):
                                        inst = t.matmul(P[1][:, 0:CAP],
                                                        wuq_sb[buf][:, dc * 256 + msq * 128: dc * 256 + msq * 128 + 128],
                                                        xt_sb[:, XG_OFF + dc * CAP: XG_OFF + dc * CAP + CAP],
                                                        start=(dc == 0), stop=(dc == DC - 1))
                                    st[("gcons", id(P[0]))] = (S, sg_pred[m])
                                    st[("gcons", id(P[1]))] = (V, hw_pred[m])
                                    if msq == 1:
                                        WGF.inc(inst)
                                        st[("wgf_at", k)] = WGF.v
                                        st[("gup", m)] = (WGF, WGF.v)
                                    else:
                                        T.inc(inst)
                                        st[("gup", m)] = (T, T.v)
                                if msq == 1:
                                    emit_gu_dma(k + 2)
                            @sec("scalar")
                            def _(s, e=e):
                                for m in range(MS):
                                    gsem, gval = st[("gup", m)]
                                    gsem.wait(s, gval)
                                    V.wait(s, hw_pred[m] - 1)
                                    inst = s.activation(out=sg_sb[:, 0:CAP], in_=P[0][:, 0:CAP], func=AF.Silu,
                                                        bias=bgt[:, e * MS + m: e * MS + m + 1])
                                    S.inc(inst)
                            assert S.v == s0 + MS
                            @sec("vector")
                            def _(v, e=e):
                                V.wait(v)
                                for m in range(MS):
                                    S.wait(v, sg_pred[m])
                                    V.wait(v)
                                    inst = v.scalar_tensor_tensor(out=hw_sb[:, (m % HWS) * 512: (m % HWS) * 512 + CAP],
                                                                  in0=P[1][:, 0:CAP],
                                                                  scalar=but[:, e * MS + m: e * MS + m + 1],
                                                                  in1=sg_sb[:, 0:CAP],
                                                                  op0=ALU.add, op1=ALU.mult)
                                    V.inc(inst)
                            assert V.v == v0 + MS
                            st[("gcons", id(P[0]))] = (S, s0 + MS)
                            st[("gcons", id(P[1]))] = (V, v0 + MS)

                            for q in range(NQ):
                                emit_oproj_chunk(q)
                            st[(tag, "wo_at", eg, NQ - 1)] = (WOF, st["wof_last"])
                            st["sp_bof_used"] = T.v
                            kq += NH
                        emit_tail(*pending)
                        scope_out(_sc)

                    else:
                        scope_out(_sc)
                        _sc = scope_in(f"{tag}.exp{p_idx}")
                        for e in range(E):
                            @sec("gpsimd")
                            def _(g, e=e):
                                if tag == "ffn1" and p_idx == 0 and e == 0:
                                    wait_join(g, join)
                                wf = st.get("wof_last")
                                if wf is not None:
                                    WOF.wait(g, wf)
                                for q in range(NQ):
                                    g.dma_start(out=wo_sb[:, q * 4 * D:(q + 1) * 4 * D],
                                                in_=WOQ[e, q]).then_inc(dWO.h, 16)
                            dWO.v += 16 * NQ
                            WO_LD = dWO.v

                            @sec("vector")
                            def _(v, e=e):
                                wb_tp_prev = st.get("wbtp_last")
                                if wb_tp_prev is not None:
                                    T.wait(v, wb_tp_prev)
                                V.wait(v)
                                for ti in range(ntile):
                                    inst = v.tensor_scalar(out=wb_sb[:, ti * 128:(ti + 1) * 128],
                                                           in0=ones_bf[:],
                                                           scalar1=wtok[:, ti * 8 + e: ti * 8 + e + 1],
                                                           scalar2=None, op0=ALU.mult)
                                V.inc(inst)
                                st[("wrep", e)] = V.v
                            ngrp = (ntile + 3) // 4
                            for grp in range(ngrp):
                                t0g = grp * 4
                                t1g = min(ntile, t0g + 4)
                                bank = PTQ[grp % 2]
                                @sec("tensor")
                                def _(t, e=e, grp=grp, t0g=t0g, t1g=t1g, bank=bank):
                                    if grp == 0:
                                        V.wait(t, st[("wrep", e)])
                                    pguard(t, bank)
                                    for ti in range(t0g, t1g):
                                        inst = t.matmul(bank[:, (ti - t0g) * 128:(ti - t0g + 1) * 128],
                                                        wb_sb[:, ti * 128:(ti + 1) * 128], idn_bf[:],
                                                        start=True, stop=True)
                                    T.inc(inst)
                                    st[("wbtp", grp)] = T.v
                                    st["wbtp_last"] = T.v
                                @sec("scalar")
                                def _(s, e=e, grp=grp, t0g=t0g, t1g=t1g, bank=bank):
                                    T.wait(s, st[("wbtp", grp)])
                                    inst = s.copy(out=wb_sb[:, t0g * 128:t1g * 128],
                                                  in_=bank[:, 0:(t1g - t0g) * 128])
                                    S.inc(inst)
                                    pset(bank, S)
                                st[("wbev", e)] = S.v

                            nch = len(pchunks)
                            def emit_opev(mh_, pass_off=pass_off, pchunks=pchunks):
                                @sec("vector")
                                def _(v, mh_=mh_):
                                  V.wait(v)
                                  OROT = [P[4], P[5], P[2]]
                                  for dc in range(DC):
                                    ops = OROT[dc % 3]
                                    for ci, (co, cw) in enumerate(pchunks):
                                        xoff = pass_off + co
                                        osem, oval = st[("op", mh_, ci, dc)]
                                        osem.wait(v, oval)
                                        inst = v.tensor_tensor(out=x_fm[:, dc * TOK + xoff: dc * TOK + xoff + cw],
                                                               in0=x_fm[:, dc * TOK + xoff: dc * TOK + xoff + cw],
                                                               in1=ops[:, 0:cw], op=ALU.add)
                                        V.inc(inst)
                                pset(P[4], V)
                                pset(P[5], V)
                            for mh in range(2):
                              MH = MS // 2
                              s0 = S.v
                              v0 = V.v
                              sg_pred = {}
                              hw_pred = {}
                              idx = 0
                              for mloc in range(MH):
                                m = mh * MH + mloc
                                for ci in range(nch):
                                    idx += 1
                                    sg_pred[(m, ci)] = s0 + idx
                                    hw_pred[(m, ci)] = v0 + 2 * idx
                              for mloc in range(MH):
                                m = mh * MH + mloc
                                q, msq = divmod(m, 4)
                                k = kq + q
                                buf = k % 2
                                @sec("tensor")
                                def _(t, m=m, msq=msq, k=k, buf=buf, e=e, mh=mh, mloc=mloc):
                                    if m == 0:
                                        S.wait(t, st[("wbev", e)])
                                        V.wait(t, WTOK_ALL)
                                    ldsem, ldval = st[("gu_ld", k)]
                                    ldsem.wait(t, ldval)
                                    for ci, (co, cw) in enumerate(pchunks):
                                        gps = P[0] if ci % 2 == 0 else P[2]
                                        ups = P[1] if ci % 2 == 0 else P[3]
                                        pguard(t, gps)
                                        pguard(t, ups)
                                        gw = st.get(("gcons", id(gps)))
                                        if gw is not None:
                                            gw[0].wait(t, gw[1])
                                        for dc in range(DC):
                                            t.matmul(gps[:, 0:cw],
                                                     wgq_sb[buf][:, dc * 512 + msq * 128: dc * 512 + msq * 128 + 128],
                                                     xt_sb[:, dc * PMAX + co: dc * PMAX + co + cw],
                                                     start=(dc == 0), stop=(dc == DC - 1))
                                        uw = st.get(("gcons", id(ups)))
                                        if uw is not None:
                                            uw[0].wait(t, uw[1])
                                        for dc in range(DC):
                                            inst = t.matmul(ups[:, 0:cw],
                                                            wuq_sb[buf][:, dc * 512 + msq * 128: dc * 512 + msq * 128 + 128],
                                                            xt_sb[:, dc * PMAX + co: dc * PMAX + co + cw],
                                                            start=(dc == 0), stop=(dc == DC - 1))
                                        st[("gcons", id(gps))] = (S, sg_pred[(m, ci)])
                                        st[("gcons", id(ups))] = (V, hw_pred[(m, ci)])
                                        if msq == 3 and ci == nch - 1:
                                            WGF.inc(inst)
                                            st[("wgf_at", k)] = WGF.v
                                            st[("gup", m, ci)] = (WGF, WGF.v)
                                        else:
                                            T.inc(inst)
                                            st[("gup", m, ci)] = (T, T.v)
                                if msq == 3:
                                    emit_gu_dma(k + 2)
                              @sec("scalar")
                              def _(s, mh=mh, e=e, pchunks=pchunks):
                                for mloc in range(MS // 2):
                                  m = mh * (MS // 2) + mloc
                                  for ci, (co, cw) in enumerate(pchunks):
                                    gps = P[0] if ci % 2 == 0 else P[2]
                                    gsem, gval = st[("gup", m, ci)]
                                    gsem.wait(s, gval)
                                    inst = s.activation(out=hw_sb[:, mloc * PMAX + co: mloc * PMAX + co + cw],
                                                        in_=gps[:, 0:cw], func=AF.Silu,
                                                        bias=bgt[:, e * MS + m: e * MS + m + 1])
                                    S.inc(inst)
                                    st[("sg", m, ci)] = S.v
                              assert S.v == s0 + idx, (S.v, s0, idx)
                              @sec("vector")
                              def _(v, mh=mh, e=e, pchunks=pchunks):
                                V.wait(v)
                                for mloc in range(MS // 2):
                                  m = mh * (MS // 2) + mloc
                                  for ci, (co, cw) in enumerate(pchunks):
                                    ups = P[1] if ci % 2 == 0 else P[3]
                                    S.wait(v, st[("sg", m, ci)])
                                    V.wait(v)
                                    vdep(v, v.scalar_tensor_tensor(out=tmp32[1][:, 0:cw], in0=ups[:, 0:cw],
                                                           scalar=but[:, e * MS + m: e * MS + m + 1],
                                                           in1=hw_sb[:, mloc * PMAX + co: mloc * PMAX + co + cw],
                                                           op0=ALU.add, op1=ALU.mult))
                                    inst = v.tensor_tensor(out=hw_sb[:, mloc * PMAX + co: mloc * PMAX + co + cw],
                                                           in0=tmp32[1][:, 0:cw],
                                                           in1=wb_sb[:, co:co + cw], op=ALU.mult)
                                    V.inc(inst)
                                    st[("hw", m, ci)] = V.v
                              assert V.v == v0 + 2 * idx, (V.v, v0, idx)
                              pset(P[0], S, st[("sg", mh * MH + MH - 1, 0)])
                              if nch > 1:
                                  pset(P[2], S, st[("sg", mh * MH + MH - 1, 1)])
                              pset(P[1], V, st[("hw", mh * MH + MH - 1, 0)])
                              if nch > 1:
                                  pset(P[3], V, st[("hw", mh * MH + MH - 1, 1)])

                              opev_base = V.v
                              OROT = [P[4], P[5], P[2]]
                              for dc in range(DC):
                                    ops = OROT[dc % 3]
                                    @sec("tensor")
                                    def _(t, dc=dc, ops=ops, e=e, mh=mh, opev_base=opev_base):
                                        if dc == 0:
                                            if mh == 0:
                                                dWO.wait(t, WO_LD)
                                                S.wait(t, WHALF_ALL)
                                            V.wait(t, st[("hw", mh * (MS // 2) + MS // 2 - 1, nch - 1)])
                                        for ci, (co, cw) in enumerate(pchunks):
                                            pguard(t, ops)
                                            first = True
                                            if e == 0 and mh == 0:
                                                t.matmul(ops[:, 0:cw], bo_sb[:, dc * 128:(dc + 1) * 128],
                                                         whalf[:, co:co + cw], start=True, stop=False)
                                                first = False
                                            for ml in range(MS // 2):
                                                mm_ = mh * (MS // 2) + ml
                                                q2, ms2 = divmod(mm_, 4)
                                                inst = t.matmul(
                                                    ops[:, 0:cw],
                                                    wo_sb[:, q2 * 4 * D + ms2 * D + dc * 128: q2 * 4 * D + ms2 * D + dc * 128 + 128],
                                                    hw_sb[:, ml * PMAX + co: ml * PMAX + co + cw],
                                                    start=(first and ml == 0), stop=(ml == MS // 2 - 1))
                                            if mh == 1 and ci == nch - 1 and dc == DC - 1:
                                                WOF.inc(inst)
                                                st["wof_last"] = WOF.v
                                                st[("op", mh, ci, dc)] = (WOF, WOF.v)
                                            else:
                                                T.inc(inst)
                                                st[("op", mh, ci, dc)] = (T, T.v)
                                            st[("pg", id(ops))] = (V, opev_base + dc * nch + ci + 1)
                              emit_opev(mh)
                            kq += NQ
                        scope_out(_sc)

                    pass_off += ptok
            st[f"{tag}_done_V"] = V.v

        def emit_attn(join):
            with ExitStack() as aes:
                AE = aes.enter_context
                xt2 = AE(nc.sbuf_tensor("xt2", [128, DC * TOK], BF16))
                kp = AE(nc.sbuf_tensor("kp", [128, TOK], BF16))
                vp = AE(nc.sbuf_tensor("vp", [128, TOK], BF16))
                qp = AE(nc.sbuf_tensor("qp", [128, TOK], BF16))
                ap_sb = AE(nc.sbuf_tensor("ap_sb", [128, TOK], BF16))
                wl = [AE(nc.sbuf_tensor(f"wl{i}", [128, DC * 128], BF16)) for i in range(2)]
                rh = [AE(nc.sbuf_tensor(f"rh{i}", [NB, TOK], BF16)) for i in range(2)]
                esb = [AE(nc.sbuf_tensor(f"esb{i}", [128, SROW], BF16)) for i in range(2)]
                diag = [AE(nc.sbuf_tensor(f"diag{i}", [128, 128], BF16)) for i in range(2)]
                gsb = [AE(nc.sbuf_tensor(f"gsb{i}", [128, SPAN], BF16)) for i in range(2)]
                lsb = [AE(nc.sbuf_tensor(f"lsb{i}", [128, SPAN], F32)) for i in range(2)]
                psb = [AE(nc.sbuf_tensor(f"psb{i}", [128, max(SPAN, 512)], BF16)) for i in range(2)]
                ptsb = AE(nc.sbuf_tensor("ptsb", [128, SPAN], BF16))
                smx = AE(nc.sbuf_tensor("smx", [128, 8], F32))
                ones_row = AE(nc.sbuf_tensor("ones_row", [1, 512], BF16))
                bpj = AE(nc.sbuf_tensor("bpj", [1, D], BF16))
                bndr = AE(nc.sbuf_tensor("bndr", [NB, WIN], BF16))

                allchunks = chunks_of(TOK)
                NCH = len(allchunks)
                _asc = scope_in("attn.ln2")
                for ci, (co, cw) in enumerate(allchunks):
                    for dc in range(DC):
                        @sec("scalar")
                        def _(s, ci=ci, dc=dc, co=co, cw=cw):
                            if ci == 0 and dc == 0:
                                dX.wait(s, X_READY)
                                dXg.wait(s, XG_READY)
                                wait_join(s, join)
                            k = ci * DC + dc
                            wk = st.get(("a_sqfree", k - 2))
                            if wk is not None:
                                T.wait(s, wk)
                            if dc == 0:
                                V.wait(s)
                            inst = s.activation(out=sqt[dc % 2][:, 0:cw],
                                                in_=x_fm[:, dc * TOK + co: dc * TOK + co + cw],
                                                func=AF.Square)
                            S.inc(inst)
                            st[("a_sq", ci, dc)] = S.v
                        @sec("tensor")
                        def _(t, ci=ci, dc=dc, co=co, cw=cw):
                            if ci == 0 and dc == 0:
                                dX.wait(t, X_READY)
                                dXg.wait(t, XG_READY)
                                V.wait(t, CONST_READY)
                                wait_join(t, join)
                                pguard(t, P[4]); pguard(t, P[5])
                            t.matmul(P[4][0:1, 0:cw], onesK_f[:, :],
                                     x_fm[:, dc * TOK + co: dc * TOK + co + cw],
                                     start=(dc == 0), stop=(dc == DC - 1))
                            S.wait(t, st[("a_sq", ci, dc)])
                            inst = t.matmul(P[5][0:1, 0:cw], onesK_f[:, :], sqt[dc % 2][:, 0:cw],
                                            start=(dc == 0), stop=(dc == DC - 1))
                            T.inc(inst)
                            st[("a_sqfree", ci * DC + dc)] = T.v
                    st[("a_stats", ci)] = T.v
                    @sec("vector")
                    def _(v, ci=ci, cw=cw):
                        T.wait(v, st[("a_stats", ci)])
                        V.wait(v)
                        vdep(v, v.tensor_scalar(out=stat[0:1, 0:cw], in0=P[4][0:1, 0:cw],
                                        scalar1=1.0 / D, scalar2=None, op0=ALU.mult))
                        vdep(v, v.tensor_scalar(out=stat[0:1, SG:SG + cw], in0=P[5][0:1, 0:cw],
                                        scalar1=1.0 / D, scalar2=None, op0=ALU.mult))
                        vdep(v, v.tensor_tensor(out=tmp32[0][0:1, 0:cw], in0=stat[0:1, 0:cw],
                                        in1=stat[0:1, 0:cw], op=ALU.mult))
                        vdep(v, v.tensor_tensor(out=stat[0:1, SG:SG + cw], in0=stat[0:1, SG:SG + cw],
                                        in1=tmp32[0][0:1, 0:cw], op=ALU.subtract))
                        inst = v.tensor_scalar(out=stat[0:1, SG:SG + cw], in0=stat[0:1, SG:SG + cw],
                                               scalar1=cfg.EPS, scalar2=None, op0=ALU.add)
                        V.inc(inst)
                        st[("a_var", ci)] = V.v
                        pset(P[4], V); pset(P[5], V)
                    @sec("scalar")
                    def _(s, ci=ci, cw=cw):
                        V.wait(s, st[("a_var", ci)])
                        inst = s.activation(out=stat[0:1, SG:SG + cw], in_=stat[0:1, SG:SG + cw], func=AF.Sqrt)
                        S.inc(inst)
                        st[("a_sqrt", ci)] = S.v
                    @sec("vector")
                    def _(v, ci=ci, cw=cw):
                        S.wait(v, st[("a_sqrt", ci)])
                        inst = v.reciprocal(stat[0:1, SG:SG + cw], stat[0:1, SG:SG + cw])
                        V.inc(inst)
                        st[("a_rstd", ci)] = V.v
                    @sec("tensor")
                    def _(t, ci=ci, cw=cw):
                        V.wait(t, st[("a_rstd", ci)])
                        pguard(t, P[0]); pguard(t, P[1])
                        t.matmul(P[0][:, 0:cw], ones1_f[:, :], stat[0:1, 0:cw], start=True, stop=True)
                        inst = t.matmul(P[1][:, 0:cw], ones1_f[:, :], stat[0:1, SG:SG + cw], start=True, stop=True)
                        T.inc(inst)
                        st[("a_bcast", ci)] = T.v
                    @sec("vector")
                    def _(v, ci=ci, co=co, cw=cw):
                        T.wait(v, st[("a_bcast", ci)])
                        V.wait(v)
                        for dc in range(DC):
                            vdep(v, v.tensor_tensor(out=tmp32[dc % 2][:, 0:cw],
                                            in0=x_fm[:, dc * TOK + co: dc * TOK + co + cw],
                                            in1=P[0][:, 0:cw], op=ALU.subtract))
                            inst = v.tensor_tensor(out=xt2[:, dc * TOK + co: dc * TOK + co + cw],
                                                   in0=tmp32[dc % 2][:, 0:cw],
                                                   in1=P[1][:, 0:cw], op=ALU.mult)
                            vdep(v, inst)
                        pset(P[0], V); pset(P[1], V)
                XT2_READY = V.v

                scope_out(_asc)
                _asc = scope_in("attn.r")
                @sec("gpsimd")
                def _(g):
                    wait_join(g, join)
                    g.dma_start(out=bpj[:], in_=BPJ).then_inc(dL.h, 16)
                    g.dma_start(out=bndr[:], in_=BNDR).then_inc(dL.h, 16)
                    g.dma_start(out=wl[0][:], in_=WRL[0]).then_inc(dL.h, 16)
                    g.dma_start(out=wl[1][:], in_=WRL[1]).then_inc(dL.h, 16)
                dL.v += 64
                WRL_RD = dL.v
                for oc in range(2):
                    for ci, (co, cw) in enumerate(allchunks):
                        ops = P[4] if (oc * NCH + ci) % 2 == 0 else P[5]
                        @sec("tensor")
                        def _(t, oc=oc, ci=ci, co=co, cw=cw, ops=ops):
                            if oc == 0 and ci == 0:
                                dL.wait(t, WRL_RD)
                                V.wait(t, XT2_READY)
                            pguard(t, ops)
                            for dc in range(DC):
                                inst = t.matmul(ops[:, 0:cw],
                                                wl[oc][:, dc * 128:(dc + 1) * 128],
                                                xt2[:, dc * TOK + co: dc * TOK + co + cw],
                                                start=(dc == 0), stop=(dc == DC - 1))
                            T.inc(inst)
                            st[("r_mm", oc, ci)] = T.v
                        @sec("scalar")
                        def _(s, oc=oc, ci=ci, cw=cw, ops=ops):
                            T.wait(s, st[("r_mm", oc, ci)])
                            dS.wait(s)
                            inst = s.activation(out=esb[0][:, 0:cw], in_=ops[:, 0:cw],
                                                func=AF.Identity, bias=brt[:, oc:oc + 1])
                            S.inc(inst)
                            pset(ops, S)
                            st[("r_ev", oc, ci)] = S.v
                        @sec("sync")
                        def _(sy, oc=oc, ci=ci, co=co, cw=cw):
                            S.wait(sy, st[("r_ev", oc, ci)])
                            sy.dma_start(out=RSCR[oc * 128:(oc + 1) * 128, co:co + cw],
                                         in_=esb[0][:, 0:cw]).then_inc(dS.h, 16)
                        dS.v += 16
                R_STORED = dS.v

                @sec("vector")
                def _(v):
                    dS.wait(v, R_STORED)
                    v.memset(ones_row[:], 1.0)
                    v.memset(esb[0][:], -1e9)
                    inst = v.memset(esb[1][:], -1e9)
                    V.inc(inst)
                ESB_INIT = V.v

                scope_out(_asc)
                for pr in range(NPAIR):
                    @sec("gpsimd")
                    def _(g, pr=pr):
                        if pr == 0:
                            T.wait(g, st[("r_mm", 1, NCH - 1)])
                        else:
                            T.wait(g, st[("pw_done", pr - 1)])
                        g.dma_start(out=wl[0][:], in_=WKL[pr]).then_inc(dL.h, 16)
                        g.dma_start(out=wl[1][:], in_=WVL[pr]).then_inc(dL.h, 16)
                    dL.v += 32
                    KV_RD = dL.v
                    _asc = scope_in(f"attn.kvq{pr}")
                    for ci, (co, cw) in enumerate(allchunks):
                        @sec("tensor")
                        def _(t, pr=pr, ci=ci, co=co, cw=cw):
                            if ci == 0:
                                dL.wait(t, KV_RD)
                                if pr == 0:
                                    V.wait(t, XT2_READY)
                            pguard(t, P[4])
                            for dc in range(DC):
                                inst = t.matmul(P[4][:, 0:cw], wl[0][:, dc * 128:(dc + 1) * 128],
                                                xt2[:, dc * TOK + co: dc * TOK + co + cw],
                                                start=(dc == 0), stop=(dc == DC - 1))
                            T.inc(inst)
                            st[("k_mm", ci)] = T.v
                        @sec("scalar")
                        def _(s, ci=ci, co=co, cw=cw):
                            T.wait(s, st[("k_mm", ci)])
                            inst = s.activation(out=kp[:, co:co + cw], in_=P[4][:, 0:cw],
                                                func=AF.Identity, scale=0.125)
                            S.inc(inst)
                            pset(P[4], S)
                            st[("k_ev", ci)] = S.v
                    for ci, (co, cw) in enumerate(allchunks):
                        @sec("tensor")
                        def _(t, ci=ci, co=co, cw=cw):
                            pguard(t, P[5])
                            for dc in range(DC):
                                inst = t.matmul(P[5][:, 0:cw], wl[1][:, dc * 128:(dc + 1) * 128],
                                                xt2[:, dc * TOK + co: dc * TOK + co + cw],
                                                start=(dc == 0), stop=(dc == DC - 1))
                            T.inc(inst)
                            st[("v_mm", ci)] = T.v
                        @sec("scalar")
                        def _(s, ci=ci, cw=cw):
                            T.wait(s, st[("v_mm", ci)])
                            wk = st.get(("vtmp_free", ci))
                            if wk is not None:
                                T.wait(s, wk)
                            inst = s.copy(out=psb[0][:, 0:cw], in_=P[5][:, 0:cw])
                            S.inc(inst)
                            pset(P[5], S)
                            st[("vtmp_ev", ci)] = S.v
                        bank = PTQ[ci % 2]
                        @sec("tensor")
                        def _(t, ci=ci, cw=cw, bank=bank):
                            S.wait(t, st[("vtmp_ev", ci)])
                            pguard(t, bank)
                            for j in range(cw // 128):
                                inst = t.matmul(bank[:, j * 128:(j + 1) * 128],
                                                psb[0][:, j * 128:(j + 1) * 128], idn_bf[:],
                                                start=True, stop=True)
                            T.inc(inst)
                            st[("v_tp", ci)] = T.v
                            st[("vtmp_free", ci + 1)] = T.v
                        @sec("vector")
                        def _(v, ci=ci, co=co, cw=cw, bank=bank):
                            T.wait(v, st[("v_tp", ci)])
                            inst = v.tensor_copy(vp[:, co:co + cw], bank[:, 0:cw])
                            V.inc(inst)
                            pset(bank, V)
                            st[("v_ev", ci)] = V.v
                    @sec("gpsimd")
                    def _(g, pr=pr):
                        T.wait(g, st[("k_mm", NCH - 1)])
                        g.dma_start(out=wl[0][:], in_=WQL[pr]).then_inc(dL.h, 16)
                    dL.v += 16
                    Q_RD = dL.v
                    for ci, (co, cw) in enumerate(allchunks):
                        @sec("tensor")
                        def _(t, pr=pr, ci=ci, co=co, cw=cw):
                            if ci == 0:
                                dL.wait(t, Q_RD)
                            pguard(t, P[4])
                            for dc in range(DC):
                                inst = t.matmul(P[4][:, 0:cw], wl[0][:, dc * 128:(dc + 1) * 128],
                                                xt2[:, dc * TOK + co: dc * TOK + co + cw],
                                                start=(dc == 0), stop=(dc == DC - 1))
                            T.inc(inst)
                            st[("q_mm", ci)] = T.v
                        @sec("scalar")
                        def _(s, pr=pr, ci=ci, co=co, cw=cw):
                            T.wait(s, st[("q_mm", ci)])
                            inst = s.activation(out=qp[:, co:co + cw], in_=P[4][:, 0:cw],
                                                func=AF.Identity, bias=bqt[:, pr:pr + 1])
                            S.inc(inst)
                            pset(P[4], S)
                            st[("q_ev", ci)] = S.v
                    st[("pw_done", pr)] = st[("q_mm", NCH - 1)]

                    scope_out(_asc)
                    _asc = scope_in(f"attn.ec{pr}")

                    def emit_e_mm(pr_, hh, qt):
                        h = pr_ * 2 + hh
                        @sec("tensor")
                        def _(t, hh=hh, qt=qt, h=h):
                            if qt == 0:
                                rs, rv = st[("rh_rd", hh)]
                                rs.wait(t, rv)
                            wk = st.get(("e_ev", h, qt - 1))
                            if wk is not None:
                                S.wait(t, wk)
                            else:
                                pguard(t, P[4]); pguard(t, P[5])
                            for half, (ho, hwid) in enumerate(chunks_of(WIN)):
                                ops = P[4] if half % 2 == 0 else P[5]
                                inst = t.matmul(ops[:, 0:hwid], rh[hh][:, qt * 128:(qt + 1) * 128],
                                                bndr[:, ho:ho + hwid], start=True, stop=True)
                            T.inc(inst)
                            st[("e_mm", h, qt)] = T.v
                            st[("rh_used", hh)] = T.v

                    def emit_e_evac_ops(s, pr_, hh, qt, guard="st"):
                        h = pr_ * 2 + hh
                        par = qt % 2
                        wk = st.get(("e_stored", par, "thru")) if guard == "st" else guard
                        if wk is not None:
                            dE2[par].wait(s, wk)
                        V.wait(s, ESB_INIT)
                        T.wait(s, st[("e_mm", h, qt)])
                        inst = s.copy(out=esb[par][:, 128:128 + WIN], in_=PBC[:, 0:WIN])
                        S.inc(inst)
                        pset(P[4], S)
                        pset(P[5], S)
                        if guard == "st":
                            st[("e_ev", h, qt)] = S.v

                    def emit_e_store(pr_, hh, qt):
                        h = pr_ * 2 + hh
                        par = qt % 2
                        @sec("sync")
                        def _(sy, h=h, qt=qt, par=par):
                            S.wait(sy, st[("e_ev", h, qt)])
                            sy.dma_start(
                                out=ESCR[h, qt * 128 * SROW:(qt + 1) * 128 * SROW].rearrange("(a b) -> a b", b=SROW),
                                in_=esb[par][:]).then_inc(dE2[par].h, 16)
                        dE2[par].v += 16
                        st[("e_st", h, qt)] = (dE2[par], dE2[par].v)
                        st[("e_stored", par, "thru")] = dE2[par].v

                    def emit_e_qt(pr_, hh, qt):
                        emit_e_mm(pr_, hh, qt)
                        @sec("scalar")
                        def _(s, pr_=pr_, hh=hh, qt=qt):
                            emit_e_evac_ops(s, pr_, hh, qt)
                        emit_e_store(pr_, hh, qt)

                    def emit_rh(pr_, hh):
                        h = pr_ * 2 + hh
                        drp = dR2[hh]
                        @sec("gpsimd")
                        def _(g, h=h, hh=hh, drp=drp):
                            dS.wait(g, R_STORED)
                            wk = st.get(("rh_free", hh))
                            if wk is not None:
                                T.wait(g, wk)
                            g.dma_start(out=rh[hh][:], in_=RSCR[h * NB:(h + 1) * NB, :]).then_inc(drp.h, 16)
                        drp.v += 16
                        st[("rh_rd", hh)] = (drp, drp.v)

                    def emit_core(pr_, hh, eco=None):
                        h = pr_ * 2 + hh
                        hbase = 64 * hh
                        s_run = S.v
                        a_prev_final = st.get("a_guard_s")
                        eco_h = eco[0] * 2 + eco[1] if eco is not None else None
                        for i in range(QT + 1):
                            if eco is not None and i < QT:
                                s_run += 1
                                st[("e_ev", eco_h, i)] = s_run
                            if i < QT:
                                s_run += 1
                                st[("p_exp", h, i)] = s_run
                            if i > 0:
                                s_run += 1
                                st[("a_ev", hh, i - 1)] = s_run
                        s_final = s_run
                        ev_guard = {}
                        for i in range(QT + 1):
                            if eco is not None and i < QT:
                                emit_e_mm(eco[0], eco[1], i)
                                ev_guard[i] = st.get(("e_stored", i % 2, "thru"))
                            qt = i
                            kt0 = max(0, qt - WT)
                            nkt = qt - kt0 + 1
                            span = nkt * 128
                            par = qt % 2
                            if i < QT:
                                gbuf = (h * QT + qt) % 2
                                dgp = dG2[gbuf]
                                @sec("gpsimd")
                                def _(g, h=h, qt=qt, gbuf=gbuf, span=span, kt0=kt0, dgp=dgp):
                                    es_, ev_ = st[("e_st", h, qt)]
                                    es_.wait(g, ev_)
                                    wk = st.get(("g_cons", gbuf))
                                    if wk is not None:
                                        V.wait(g, wk)
                                    base = qt * 128 * (SROW - 1) + (127 + WIN) + kt0 * 128
                                    src = ESCR[h, base: base + 128 * (SROW - 1)].rearrange(
                                        "(i j) -> i j", j=SROW - 1)[:, 0:span]
                                    g.dma_start(out=gsb[gbuf][:, 0:span], in_=src).then_inc(dgp.h, 16)
                                dgp.v += 16
                                st[("g_rd", h, qt)] = (dgp, dgp.v)

                            @sec("tensor")
                            def _(t, i=i, hh=hh, h=h, hbase=hbase):
                                if i < QT:
                                    if i == 0 and hh == 0:
                                        S.wait(t, st[("q_ev", NCH - 1)])
                                        S.wait(t, st[("k_ev", NCH - 1)])
                                        V.wait(t, st[("v_ev", NCH - 1)])
                                    wk = st.get("s_psum_guard")
                                    if wk is not None:
                                        V.wait(t, wk)
                                    kt0_ = max(0, i - WT)
                                    off, rem, c_ = 0, (i - kt0_ + 1) * 128, 0
                                    while rem > 0:
                                        w = min(512, rem)
                                        inst = t.matmul(P[c_][:, 0:w],
                                                        qp[hbase:hbase + 64, i * 128:(i + 1) * 128],
                                                        kp[hbase:hbase + 64, kt0_ * 128 + off: kt0_ * 128 + off + w],
                                                        start=True, stop=True)
                                        off += w; rem -= w; c_ += 1
                                    T.inc(inst)
                                    st[("s_mm", h, i)] = T.v
                                if i > 0:
                                    qp_ = i - 1
                                    nkt_ = qp_ - max(0, qp_ - WT) + 1
                                    S.wait(t, st[("p_exp", h, qp_)])
                                    V.wait(t, st[("diag", h, qp_)])
                                    V_base = V.v
                                    for gg in range((nkt_ + 3) // 4):
                                        bank = PTQ[gg % 2]
                                        if gg >= 2:
                                            V.wait(t, V_base + gg - 1)
                                        else:
                                            pguard(t, bank)
                                        for jj in range(gg * 4, min(nkt_, gg * 4 + 4)):
                                            inst = t.matmul(bank[:, (jj - gg * 4) * 128:(jj - gg * 4 + 1) * 128],
                                                            psb[qp_ % 2][:, jj * 128:(jj + 1) * 128],
                                                            diag[qp_ % 2][:], start=True, stop=True)
                                        T.inc(inst)
                                        st[("p_tp_g", h, qp_, gg)] = T.v
                                    st[("p_tp", h, qp_)] = T.v
                                    V.wait(t, V.v + (nkt_ + 3) // 4)
                                    wk = st[("a_ev", hh, qp_ - 1)] if qp_ >= 1 else a_prev_final
                                    if wk is not None:
                                        S.wait(t, wk)
                                    kt0_ = max(0, qp_ - WT)
                                    for jj in range(nkt_):
                                        kt = kt0_ + jj
                                        inst = t.matmul(P[3][0:64, 0:128],
                                                        vp[:, kt * 128 + 64 * hh: kt * 128 + 64 * hh + 64],
                                                        ptsb[:, jj * 128:(jj + 1) * 128],
                                                        start=(jj == 0), stop=(jj == nkt_ - 1))
                                    T.inc(inst)
                                    st[("av", h, qp_)] = T.v

                            @sec("vector")
                            def _(v, i=i, h=h, hh=hh):
                                if i > 0:
                                    qp_ = i - 1
                                    nkt_ = qp_ - max(0, qp_ - WT) + 1
                                    for gg in range((nkt_ + 3) // 4):
                                        bank = PTQ[gg % 2]
                                        ww = (min(nkt_, gg * 4 + 4) - gg * 4) * 128
                                        T.wait(v, st[("p_tp_g", h, qp_, gg)])
                                        inst = v.tensor_copy(ptsb[:, gg * 512:gg * 512 + ww], bank[:, 0:ww])
                                        V.inc(inst)
                                        pset(bank, V)
                                if i < QT:
                                    qt = i
                                    par = qt % 2
                                    kt0_ = max(0, qt - WT)
                                    span_ = (qt - kt0_ + 1) * 128
                                    gbuf = (h * QT + qt) % 2
                                    T.wait(v, st[("s_mm", h, qt)])
                                    gs_, gv_ = st[("g_rd", h, qt)]
                                    gs_.wait(v, gv_)
                                    if i > 1:
                                        S.wait(v, st[("p_exp", h, i - 2)])
                                    V.wait(v)
                                    inst = v.tensor_tensor(out=lsb[par][:, 0:span_], in0=SP3[:, 0:span_],
                                                    in1=gsb[gbuf][:, 0:span_], op=ALU.add)
                                    V.inc(inst)
                                    st[("g_cons", gbuf)] = V.v
                                    st[("b_done", h, qt)] = V.v
                                    st["s_psum_guard"] = V.v
                                    S.wait(v, st[("p_exp", h, i)])
                                    if i >= 2:
                                        T.wait(v, st[("p_tp", h, i - 2)])
                                    vdep(v, v.reciprocal(smx[:, par * 4 + 2:par * 4 + 3],
                                                         smx[:, par * 4 + 1:par * 4 + 2]))
                                    inst = v.tensor_scalar(out=diag[par][:], in0=idn_bf[:],
                                                           scalar1=smx[:, par * 4 + 2:par * 4 + 3],
                                                           scalar2=None, op0=ALU.mult)
                                    V.inc(inst)
                                    st[("diag", h, qt)] = V.v

                            if eco is not None and i < QT:
                                emit_e_store(eco[0], eco[1], i)

                        @sec("scalar")
                        def _(s, h=h, hh=hh, eco=eco, eco_h=eco_h):
                            for i in range(QT + 1):
                                if eco is not None and i < QT:
                                    emit_e_evac_ops(s, eco[0], eco[1], i, guard=ev_guard[i])
                                if i < QT:
                                    qt = i
                                    par = qt % 2
                                    span_ = (qt - max(0, qt - WT) + 1) * 128
                                    V.wait(s, st[("b_done", h, qt)])
                                    if i > 1:
                                        T.wait(s, st[("p_tp", h, i - 2)])
                                    inst = s.activation(out=psb[par][:, 0:span_], in_=lsb[par][:, 0:span_], func=AF.Exp,
                                                        accum_out=smx[:, par * 4 + 1:par * 4 + 2])
                                    S.inc(inst)
                                    assert S.v == st[("p_exp", h, qt)]
                                if i > 0:
                                    qp_ = i - 1
                                    T.wait(s, st[("av", h, qp_)])
                                    inst = s.copy(out=ap_sb[64 * hh:64 * hh + 64, qp_ * 128:(qp_ + 1) * 128],
                                                  in_=P[3][0:64, 0:128])
                                    S.inc(inst)
                                    assert S.v == st[("a_ev", hh, qp_)]
                        assert S.v == s_final
                        st["a_guard_s"] = st[("a_ev", hh, QT - 1)]
                        st[("rh_free", hh)] = st[("rh_used", hh)]

                    if pr == 0:
                        emit_rh(pr, 0)
                        for qt0 in range(QT):
                            emit_e_qt(pr, 0, qt0)
                    emit_rh(pr, 1)
                    emit_core(pr, 0, eco=(pr, 1))
                    if pr + 1 < NPAIR:
                        emit_rh(pr + 1, 0)
                        emit_core(pr, 1, eco=(pr + 1, 0))
                    else:
                        emit_core(pr, 1)

                    scope_out(_asc)
                    _asc = scope_in(f"attn.proj{pr}")
                    @sec("gpsimd")
                    def _(g, pr=pr):
                        T.wait(g, st[("pw_done", pr)])
                        g.dma_start(out=wl[1][:], in_=WPL[pr]).then_inc(dL.h, 16)
                    dL.v += 16
                    WP_RD = dL.v
                    for ci, (co, cw) in enumerate(allchunks):
                        for dc in range(DC):
                            ops = P[4] if dc % 2 == 0 else P[5]
                            @sec("tensor")
                            def _(t, pr=pr, ci=ci, co=co, cw=cw, dc=dc, ops=ops):
                                if ci == 0 and dc == 0:
                                    dL.wait(t, WP_RD)
                                    S.wait(t, st[("a_ev", 1, QT - 1)])
                                pguard(t, ops)
                                first = True
                                if pr == 0:
                                    t.matmul(ops[:, 0:cw], bpj[0:1, dc * 128:(dc + 1) * 128],
                                             ones_row[0:1, 0:cw], start=True, stop=False)
                                    first = False
                                inst = t.matmul(ops[:, 0:cw], wl[1][:, dc * 128:(dc + 1) * 128],
                                                ap_sb[:, co:co + cw], start=first, stop=True)
                                T.inc(inst)
                                st[("pj", ci, dc)] = T.v
                            @sec("vector")
                            def _(v, ci=ci, co=co, cw=cw, dc=dc, ops=ops):
                                T.wait(v, st[("pj", ci, dc)])
                                V.wait(v)
                                inst = v.tensor_tensor(out=x_fm[:, dc * TOK + co: dc * TOK + co + cw],
                                                       in0=x_fm[:, dc * TOK + co: dc * TOK + co + cw],
                                                       in1=ops[:, 0:cw], op=ALU.add)
                                V.inc(inst)
                                pset(ops, V)
                    st[("pw_done", pr)] = T.v
                    scope_out(_asc)
            st["attn_done_V"] = V.v

        join0 = phase_join_waits()
        if "ffn1" in cfg.PHASES:
            emit_ffn("ffn1", join0)
        if "attn" in cfg.PHASES:
            emit_attn(phase_join_waits())
        if "ffn2" in cfg.PHASES:
            emit_ffn("ffn2", phase_join_waits())

        FINAL_V = V.v
        FINAL_T = T.v
        FINAL_S = S.v
        @sec("sync")
        def _(sy):
            V.wait(sy, FINAL_V)
            T.wait(sy, FINAL_T)
            S.wait(sy, FINAL_S)
            sy.dma_start(out=OUT, in_=x_fm[:]).then_inc(dO.h, 16)
            sy.wait_ge(dO.h, 16)
    return nc


def layout_inputs(cfg: CFG, X, Wq, bq, Wk, Wv, Wproj, bproj, Wr, br, b_nd,
                  W_router, W_gate, b_gate, W_up, b_up, W_out, b_out):
    TOK, D, E, M, NB, WIN = cfg.TOK, cfg.D, cfg.E, cfg.M, cfg.NB, cfg.WIN
    DC, MS, NQ, HEADS = cfg.DC, cfg.MS, cfg.NQ, cfg.HEADS
    NPAIR = HEADS // 2
    f32 = np.float32
    d = {}
    d["XFM"] = X.T.reshape(DC, 128, TOK).transpose(1, 0, 2).reshape(128, DC * TOK)
    NH = MS // 2
    d["WGQ"] = W_gate.reshape(E, DC, 128, NH, 256).transpose(0, 3, 2, 1, 4).reshape(E, NH, 128, 8 * 256)
    d["WUQ"] = W_up.reshape(E, DC, 128, NH, 256).transpose(0, 3, 2, 1, 4).reshape(E, NH, 128, 8 * 256)
    d["WOQ"] = W_out.reshape(E, NQ, 4, 128, D).transpose(0, 1, 3, 2, 4).reshape(E, NQ, 128, 4 * D)
    d["BGT"] = b_gate.reshape(E, MS, 128).transpose(2, 0, 1).reshape(128, E * MS)
    d["BUT"] = b_up.reshape(E, MS, 128).transpose(2, 0, 1).reshape(128, E * MS)
    d["BO"] = b_out
    d["WRT"] = W_router.reshape(DC, 128, E).transpose(1, 0, 2).reshape(128, DC * E)
    def wl_layout(W, n_oc):
        return W.reshape(DC, 128, n_oc, 128).transpose(2, 1, 0, 3).reshape(n_oc, 128, DC * 128)
    d["WQL"] = wl_layout(Wq, NPAIR)
    d["WKL"] = wl_layout(Wk, NPAIR)
    d["WVL"] = wl_layout(Wv, NPAIR)
    d["WPL"] = Wproj.reshape(NPAIR, 128, DC * 128)
    Wr_pad = np.zeros((D, 256), f32); Wr_pad[:, 0:HEADS * NB] = Wr
    d["WRL"] = wl_layout(Wr_pad, 2)
    d["BQT"] = bq.reshape(NPAIR, 128).T
    br_pad = np.zeros((256,), f32); br_pad[0:HEADS * NB] = br
    d["BRT"] = br_pad.reshape(2, 128).T
    d["BPJ"] = bproj.reshape(1, D)
    d["BNDR"] = b_nd[:, ::-1]
    d["IDN"] = np.eye(128, dtype=f32)
    d["IDNF"] = np.eye(128, dtype=f32)
    d["CIOTA"] = np.tile(np.arange(512, dtype=f32), (128, 1))
    d["CCOL"] = np.arange(128, dtype=f32)[:, None] + 128.0 * np.arange(4, dtype=f32)[None, :]
    d["TRIU"] = np.triu(np.ones((128, 128), dtype=f32), 1)
    d["BOF"] = b_out.reshape(1, E * D)
    return {k: np.ascontiguousarray(np.asarray(v, f32)) for k, v in d.items()}


def split_args(inputs):
    return dict(
        Wq=np.asarray(inputs["L0_attn_Wq"], np.float32),
        bq=np.asarray(inputs["L0_attn_bq"], np.float32),
        Wk=np.asarray(inputs["L0_attn_Wk"], np.float32),
        Wv=np.asarray(inputs["L0_attn_Wv"], np.float32),
        Wproj=np.asarray(inputs["L0_attn_Wproj"], np.float32),
        bproj=np.asarray(inputs["L0_attn_bproj"], np.float32),
        Wr=np.asarray(inputs["L0_attn_Wr"], np.float32),
        br=np.asarray(inputs["L0_attn_br"], np.float32),
        b_nd=np.asarray(inputs["L0_attn_b_nd"], np.float32),
        W_router=np.asarray(inputs["L0_ffn_W_router"], np.float32),
        W_gate=np.asarray(inputs["L0_ffn_W_gate"], np.float32),
        b_gate=np.asarray(inputs["L0_ffn_b_gate"], np.float32),
        W_up=np.asarray(inputs["L0_ffn_W_up"], np.float32),
        b_up=np.asarray(inputs["L0_ffn_b_up"], np.float32),
        W_out=np.asarray(inputs["L0_ffn_W_out"], np.float32),
        b_out=np.asarray(inputs["L0_ffn_b_out"], np.float32),
    )


def kernel(**inputs):
    cfg = CFG()
    nc = build(cfg)
    X = np.asarray(inputs["L0_attn_X"], np.float32)
    args = split_args(inputs)
    in_maps = []
    for c in range(8):
        b, h = divmod(c, 2)
        rows = X[b, 0:2560] if h == 0 else X[b, 1536:4096]
        in_maps.append(layout_inputs(cfg, np.ascontiguousarray(rows), **args))
    res = run_bass_kernel_spmd(nc, in_maps, core_ids=list(range(8)))
    out = np.empty((4, 4096, 1024), np.float32)
    for c in range(8):
        b, h = divmod(c, 2)
        r = np.asarray(res.results[c]["OUT"]).reshape(128, 8, 2560).transpose(2, 1, 0).reshape(2560, 1024)
        if h == 0:
            out[b, 0:2560] = r
        else:
            out[b, 2560:4096] = r[1024:2560]
    return out
